# version 03
# baseline (speedup 1.0000x reference)
import numpy as np
import concourse.bass as bass
import concourse.mybir as mybir
from concourse.bass_utils import run_bass_kernel_spmd

F32 = mybir.dt.float32
BF16 = mybir.dt.bfloat16
AF = mybir.ActivationFunctionType
ALU = mybir.AluOpType

D = 2048
NDC = 16
H = 16
NE = 8
GL = 0.5 - float(np.log(4.0 / 3.0))
GC = float(1.0 / np.e)
GU = 0.5
ZEPS = 1e-6
LNEPS = 1e-5

ENGINES = ("sync", "scalar", "vector", "tensor", "gpsimd")
QSEM = {"sync": "sy", "scalar": "sc", "gpsimd": "gq"}
ESEM = {"tensor": "pe", "scalar": "ac", "vector": "dv", "gpsimd": "gp", "sync": "sq"}


class Sch:
    def __init__(self):
        self.steps = {e: [] for e in ENGINES}
        self.cnt = {}
        self.marks = {e: {} for e in ENGINES}

    def do(self, eng, emit, waits=(), dma=False):
        ws = {}
        for w in waits:
            if not w:
                continue
            s, v = w
            if v <= 0:
                continue
            if eng == "tensor" and s == "pe":
                continue
            if self.marks[eng].get(s, 0) >= v:
                continue
            ws[s] = max(ws.get(s, 0), v)
        for s, v in ws.items():
            self.marks[eng][s] = v
        sem = QSEM[eng] if dma else ESEM[eng]
        amt = 16 if dma else 1
        self.cnt[sem] = self.cnt.get(sem, 0) + amt
        ev = (sem, self.cnt[sem])
        self.steps[eng].append((sorted(ws.items()), emit, sem, amt))
        return ev

    def raw(self, eng, emit):
        self.steps[eng].append(((), emit, None, 0))

    def snapshot(self):
        return list(self.cnt.items())


class Buf:
    def __init__(self, seed=()):
        self.w = {}
        self.r = {}
        for s, v in seed:
            self.r[s] = max(self.r.get(s, 0), v)

    def rdeps(self):
        return list(self.w.items())

    def wdeps(self):
        d = dict(self.w)
        for s, v in self.r.items():
            d[s] = max(d.get(s, 0), v)
        return list(d.items())

    def wrote(self, ev, reset=True):
        if reset:
            self.w = {}
            self.r = {}
        if ev:
            s, v = ev
            self.w[s] = max(self.w.get(s, 0), v)

    def read(self, ev):
        if ev:
            s, v = ev
            self.r[s] = max(self.r.get(s, 0), v)


def build_program(S=4096, debug=False):
    assert S % 512 == 0
    NG = S // 512

    nc = bass.Bass("TRN2", target_bir_lowering=False)

    x_d = nc.dram_tensor("x", [S, D], F32, kind="ExternalInput").ap()
    qw_d = nc.dram_tensor("qw", [D, D], F32, kind="ExternalInput").ap()
    kw_d = nc.dram_tensor("kw", [D, D], F32, kind="ExternalInput").ap()
    vw_d = nc.dram_tensor("vw", [D, D], F32, kind="ExternalInput").ap()
    ow_d = nc.dram_tensor("ow", [D, D], F32, kind="ExternalInput").ap()
    qb_d = nc.dram_tensor("qb", [1, D], F32, kind="ExternalInput").ap()
    kb_d = nc.dram_tensor("kb", [1, D], F32, kind="ExternalInput").ap()
    vb_d = nc.dram_tensor("vb", [1, D], F32, kind="ExternalInput").ap()
    ob_d = nc.dram_tensor("ob", [1, D], F32, kind="ExternalInput").ap()
    wg_d = nc.dram_tensor("wg", [NE, D, D], F32, kind="ExternalInput").ap()
    wu_d = nc.dram_tensor("wu", [NE, D, D], F32, kind="ExternalInput").ap()
    wd_d = nc.dram_tensor("wd", [NE, D, D], F32, kind="ExternalInput").ap()
    wr_d = nc.dram_tensor("wr", [D, NE], F32, kind="ExternalInput").ap()
    temp_d = nc.dram_tensor("temp", [1, 1], F32, kind="ExternalInput").ap()
    sel_d = nc.dram_tensor("sel", [16, D], F32, kind="ExternalInput").ap()
    out_d = nc.dram_tensor("out", [S, D], F32, kind="ExternalOutput").ap()
    y_d = nc.dram_tensor("yscr", [S, D], F32, kind="Internal").ap()
    qwT_d = nc.dram_tensor("qwTs", [D, D], BF16, kind="Internal").ap()
    kwT_d = nc.dram_tensor("kwTs", [D, D], BF16, kind="Internal").ap()
    vwT_d = nc.dram_tensor("vwTs", [D, D], BF16, kind="Internal").ap()
    owT_d = nc.dram_tensor("owTs", [D, D], BF16, kind="Internal").ap()
    wgs_d = nc.dram_tensor("wgs", [NE, D, D], BF16, kind="Internal").ap()
    wus_d = nc.dram_tensor("wus", [NE, D, D], BF16, kind="Internal").ap()
    wds_d = nc.dram_tensor("wds", [NE, D, D], BF16, kind="Internal").ap()
    if debug:
        dbg_kvz = nc.dram_tensor("dbg_kvz", [128, D], F32, kind="ExternalOutput").ap()
        dbg_ks = nc.dram_tensor("dbg_ks", [128, 16], F32, kind="ExternalOutput").ap()
        dbg_y = nc.dram_tensor("dbg_y", [S, D], F32, kind="ExternalOutput").ap()

    cur = [20480]
    hwm = [20480]
    offs = {}

    def place(name, shape, dtype, at=None):
        nbytes = int(np.prod(shape[1:])) * (2 if dtype == BF16 else 4)
        if at is None:
            off = (cur[0] + 63) & ~63
            cur[0] = off + nbytes
        else:
            off = at
        offs[name] = off
        hwm[0] = max(hwm[0], off + nbytes)
        return nc.alloc_sbuf_tensor_at(name, list(shape), dtype, offset=off).ap()

    ones_row = place("ones_row", [1, 512], BF16)
    ones_f32 = place("ones_f32", [1, 128], F32)
    onesc = place("onesc", [128, 1], BF16)
    bias_k = place("bias_k", [1, D], BF16)
    bias_v = place("bias_v", [1, D], BF16)
    bias_q = place("bias_q", [1, D], BF16)
    bias_o = place("bias_o", [1, D], BF16)
    wrb = place("wrb", [128, NDC * NE], BF16)
    invt = place("invt", [128, 1], F32)
    t11 = place("t11", [1, 1], F32)
    selb = place("selb", [16, D], BF16)
    ksum_bd = place("ksum_bd", [128, 256], BF16)
    kvz = place("kvz", [128, D], BF16)
    ksum_acc = place("ksum_acc", [128, 16], F32)
    sA = place("sA", [128, 1], F32)
    sB = place("sB", [128, 1], F32)
    st0 = place("st0", [128, 1], F32)
    st1 = place("st1", [128, 1], F32)
    st2 = place("st2", [128, 1], F32)
    rstd = place("rstd", [128, 1], F32)
    negmr = place("negmr", [128, 1], F32)
    r_t0 = place("r_t0", [128, NE], F32)
    r_t1 = place("r_t1", [128, NE], F32)
    r_t2 = place("r_t2", [128, NE], F32)
    r_inh = place("r_inh", [128, NE], F32)
    r_c0 = place("r_c0", [128, 1], F32)
    r_c1 = place("r_c1", [128, 1], F32)
    w_all = [place(f"w_all{i}", [128, 4 * NE], F32) for i in range(2)]

    STREAM = (cur[0] + 63) & ~63
    kwb = [place(f"kwb{i}", [128, NDC * 512], BF16, STREAM + i * 16384) for i in range(2)]
    vwb = [place(f"vwb{i}", [128, NDC * 512], BF16, STREAM + (2 + i) * 16384) for i in range(2)]
    qwb = [place(f"qwb{i}", [128, NDC * 512], BF16, STREAM + i * 16384) for i in range(2)]
    owb = [place(f"owb{i}", [128, NDC * 512], BF16, STREAM + (2 + i) * 16384) for i in range(4)]
    wgb = [place(f"wgb{i}", [128, NDC * 512], BF16, STREAM + i * 16384) for i in range(2)]
    wub = [place(f"wub{i}", [128, NDC * 512], BF16, STREAM + (2 + i) * 16384) for i in range(2)]
    wdb = [place(f"wdb{i}", [128, NDC * 512], BF16, STREAM + (4 + i) * 16384) for i in range(2)]
    WORK = STREAM + 6 * 16384

    wstg = [place(f"wstg{i}", [128, D], BF16, WORK + i * 4096) for i in range(2)]
    wtt = [place(f"wtt{i}", [128, NDC * 128], BF16, WORK + 8192 + i * 4096) for i in range(2)]
    x_bf = [place(f"x_bf{i}", [128, D], BF16, WORK + 16384 + i * 4096) for i in range(2)]
    xT = [place(f"xT{i}", [128, NDC * 128], BF16, WORK + 24576 + i * 4096) for i in range(4)]
    kp_t = [place(f"kp_t{i}", [128, D], BF16, WORK + 40960 + i * 4096) for i in range(4)]
    v_t = [place(f"v_t{i}", [128, D], BF16, WORK + 57344 + i * 4096) for i in range(4)]
    e_scr = place("e_scr", [128, 512], F32, STREAM + 4 * 16384)
    kv_acc = place("kv_acc", [128, D], F32, STREAM + 4 * 16384 + 2048)
    xB_bf = [place(f"xB_bf{i}", [128, D], BF16, WORK + i * 4096) for i in range(2)]
    xT_fm = place("xT_fm", [128, NDC * 512], BF16, WORK + 8192)
    qpT = place("qpT", [128, NDC * 512], BF16, WORK + 24576)
    attnT = place("attnT", [128, NDC * 512], BF16, WORK + 40960)
    eB_scr = place("eB_scr", [128, 512], F32, WORK + 57344)
    zs = place("zs", [16, 512], F32, WORK + 59392)
    zrec_bf = place("zrec_bf", [16, 512], BF16, WORK + 61440)
    x_f32 = place("x_f32", [128, D], F32, WORK + 62464)
    y_f32 = place("y_f32", [128, D], F32, WORK + 70656)
    assert offs["bias_v"] == offs["bias_k"] + 4096 and offs["bias_o"] == offs["bias_k"] + 12288
    yT = [place("yT0", [128, NDC * 512], BF16, WORK),
          place("yT1", [128, NDC * 512], BF16, offs["bias_k"])]
    y_bf = place("y_bf", [128, D], BF16, WORK + 16384)
    out_acc = place("out_acc", [128, 4 * D], F32, WORK + 20480)
    h_sb = place("h_sb", [128, 16 * 512], BF16, WORK + 53248)
    s_sb = place("s_sb", [128, 512], BF16, WORK + 69632)
    o_f32 = place("o_f32", [128, D], F32, WORK + 70656)
    assert max(hwm[0], WORK + 78848) <= 229376, (hwm[0], WORK)

    ps = [nc.alloc_psum_tensor(f"ps{i}", [128, 512], F32).ap() for i in range(8)]

    sc = Sch()
    bankb = [Buf() for _ in range(8)]
    bufs = {}
    cur_seed = [()]

    def B(name):
        if name not in bufs:
            bufs[name] = Buf(cur_seed[0])
        return bufs[name]

    def barrier():
        snap = sc.snapshot()
        cur_seed[0] = snap
        bufs.clear()
        for i in range(8):
            bankb[i] = Buf(snap)

    def dma(queue, out, in_, waits=(), transpose=False):
        if transpose:
            return sc.do(queue, lambda e, o=out, i=in_: e.dma_start(out=o, in_=i, transpose=True),
                         waits=waits, dma=True)
        return sc.do(queue, lambda e, o=out, i=in_: e.dma_start(out=o, in_=i),
                     waits=waits, dma=True)

    def mm_group(mms, bank_i, extra_waits=()):
        bb = bankb[bank_i]
        w = bb.wdeps() + list(extra_waits)
        ev = None
        for i, m in enumerate(mms):
            if len(mms) == 1:
                ev = sc.do("tensor", m, waits=w)
            elif i == 0:
                sc.do("tensor", m, waits=w)
            elif i == len(mms) - 1:
                ev = sc.do("tensor", m)
            else:
                sc.raw("tensor", m)
        bb.wrote(ev, reset=True)
        return ev

    def ln_apply(src_ap, dst_ap, src_buf, dst_buf):
        w0 = B("sA").rdeps() + B("sB").rdeps() + B("st0").wdeps() + B("st1").wdeps() + B("st2").wdeps()
        ev = sc.do("vector", lambda e: e.tensor_scalar(st0[:], sA[:], 1.0 / D, None, ALU.mult), waits=w0)
        B("sA").read(ev)
        B("st0").wrote(ev)
        ev = sc.do("vector", lambda e: e.tensor_scalar(st1[:], sB[:], 1.0 / D, None, ALU.mult),
                   waits=B("st1").wdeps())
        B("sB").read(ev)
        B("st1").wrote(ev)
        ev = sc.do("vector", lambda e: e.tensor_tensor(st2[:], st0[:], st0[:], ALU.mult),
                   waits=B("st0").rdeps() + B("st2").wdeps())
        B("st2").wrote(ev)
        ev = sc.do("vector", lambda e: e.tensor_tensor(st1[:], st1[:], st2[:], ALU.subtract),
                   waits=B("st1").rdeps() + B("st2").rdeps())
        B("st2").read(ev)
        B("st1").wrote(ev, reset=False)
        ev = sc.do("vector", lambda e: e.tensor_scalar(st1[:], st1[:], LNEPS, None, ALU.add),
                   waits=B("st1").rdeps())
        B("st1").wrote(ev, reset=False)
        eva = sc.do("scalar", lambda e: e.activation(out=st2[:], in_=st1[:], func=AF.Sqrt),
                    waits=B("st1").rdeps() + B("st2").wdeps())
        B("st1").read(eva)
        B("st2").wrote(eva)
        evr = sc.do("vector", lambda e: e.reciprocal(rstd[:], st2[:]),
                    waits=B("st2").rdeps() + B("rstd").wdeps())
        B("st2").read(evr)
        B("rstd").wrote(evr)
        evn = sc.do("vector", lambda e: e.tensor_scalar(negmr[:], st0[:], rstd[:], -1.0, ALU.mult, ALU.mult),
                    waits=B("st0").rdeps() + B("rstd").rdeps() + B("negmr").wdeps())
        B("st0").read(evn)
        B("rstd").read(evn)
        B("negmr").wrote(evn)
        evo = sc.do("vector", lambda e: e.tensor_scalar(dst_ap, src_ap, rstd[:], negmr[:], ALU.mult, ALU.add),
                    waits=B("rstd").rdeps() + B("negmr").rdeps() + B(src_buf).rdeps()
                    + (B(dst_buf).wdeps() if dst_buf != src_buf else []))
        B("rstd").read(evo)
        B("negmr").read(evo)
        B(src_buf).read(evo)
        B(dst_buf).wrote(evo, reset=(dst_buf != src_buf))
        return evo

    def router(psum_ap, w_dst_ap, first, wname="w_all"):
        ev = sc.do("scalar", lambda e: e.activation(out=r_inh[:], in_=psum_ap, func=AF.Sigmoid,
                                                    scale=invt[:]),
                   waits=bankb[2].rdeps() + B("invt").rdeps() + B("r_inh").wdeps())
        bankb[2].read(ev)
        B("r_inh").wrote(ev)

        def dv(emit, reads=(), writes=()):
            w = []
            for nm in reads:
                w += B(nm).rdeps()
            for nm in writes:
                w += B(nm).wdeps()
            ev = sc.do("vector", emit, waits=w)
            for nm in reads:
                B(nm).read(ev)
            for nm in writes:
                B(nm).wrote(ev)
            return ev

        dv(lambda e: e.tensor_scalar(r_t0[:], r_inh[:], GL, None, ALU.is_ge), ("r_inh",), ("r_t0",))
        dv(lambda e: e.tensor_scalar(r_t1[:], r_inh[:], GU, None, ALU.is_le), ("r_inh",), ("r_t1",))
        dv(lambda e: e.tensor_tensor(r_t0[:], r_t0[:], r_t1[:], ALU.mult), ("r_t0", "r_t1"), ("r_t0",))
        dv(lambda e: e.tensor_scalar(r_t1[:], r_inh[:], GC, None, ALU.subtract), ("r_inh",), ("r_t1",))
        dv(lambda e: e.tensor_scalar(r_t2[:], r_t1[:], -1.0, None, ALU.mult), ("r_t1",), ("r_t2",))
        dv(lambda e: e.tensor_tensor(r_t1[:], r_t1[:], r_t2[:], ALU.max), ("r_t1", "r_t2"), ("r_t1",))
        evd = dv(lambda e: e.tensor_scalar(r_t2[:], r_t1[:], -10.0, None, ALU.mult), ("r_t1",), ("r_t2",))
        evd2 = dv(lambda e: e.tensor_scalar(r_t1[:], r_t1[:], -(1.0 / 0.3), None, ALU.mult),
                  ("r_t1",), ("r_t1",))
        eva = sc.do("scalar", lambda e: e.activation(out=r_t2[:], in_=r_t2[:], func=AF.Exp),
                    waits=B("r_t2").rdeps())
        B("r_t2").wrote(eva, reset=False)
        evb = sc.do("scalar", lambda e: e.activation(out=r_inh[:], in_=r_t1[:], func=AF.Exp),
                    waits=B("r_t1").rdeps() + B("r_inh").wdeps())
        B("r_t1").read(evb)
        B("r_inh").wrote(evb)
        dv(lambda e: e.tensor_tensor(r_t2[:], r_t2[:], r_t0[:], ALU.mult), ("r_t2", "r_t0"), ("r_t2",))
        dv(lambda e: e.reduce_sum(r_c0[:], r_t2[:], axis=mybir.AxisListType.X), ("r_t2",), ("r_c0",))
        dv(lambda e: e.reduce_max(r_c1[:], r_inh[:], axis=mybir.AxisListType.X), ("r_inh",), ("r_c1",))
        dv(lambda e: e.tensor_scalar(r_t0[:], r_inh[:], r_c1[:], None, ALU.is_ge),
           ("r_inh", "r_c1"), ("r_t0",))
        dv(lambda e: e.scalar_tensor_tensor(r_t1[:], r_t0[:], -1e9, r_inh[:], ALU.mult, ALU.add),
           ("r_t0", "r_inh"), ("r_t1",))
        dv(lambda e: e.reduce_max(r_c1[:], r_t1[:], axis=mybir.AxisListType.X), ("r_t1",), ("r_c1",))
        dv(lambda e: e.tensor_scalar(r_t1[:], r_t1[:], r_c1[:], None, ALU.is_ge),
           ("r_t1", "r_c1"), ("r_t1",))
        dv(lambda e: e.tensor_tensor(r_t0[:], r_t0[:], r_t1[:], ALU.add), ("r_t0", "r_t1"), ("r_t0",))
        dv(lambda e: e.tensor_tensor(r_t0[:], r_t0[:], r_inh[:], ALU.mult), ("r_t0", "r_inh"), ("r_t0",))
        dv(lambda e: e.reduce_sum(r_c1[:], r_t0[:], axis=mybir.AxisListType.X), ("r_t0",), ("r_c1",))
        dv(lambda e: e.tensor_scalar(r_c1[:], r_c1[:], 1e-8, None, ALU.max), ("r_c1",), ("r_c1",))
        dv(lambda e: e.reciprocal(r_c1[:], r_c1[:]), ("r_c1",), ("r_c1",))
        dv(lambda e: e.tensor_scalar(r_t0[:], r_t0[:], r_c1[:], None, ALU.mult), ("r_t0", "r_c1"), ("r_t0",))
        dv(lambda e: e.tensor_scalar(r_c1[:], r_c0[:], 1e-8, None, ALU.is_lt), ("r_c0",), ("r_c1",))
        dv(lambda e: e.tensor_tensor(r_t0[:], r_t0[:], r_t2[:], ALU.subtract), ("r_t0", "r_t2"), ("r_t0",))
        dv(lambda e: e.tensor_scalar(r_t0[:], r_t0[:], r_c1[:], None, ALU.mult), ("r_t0", "r_c1"), ("r_t0",))
        dv(lambda e: e.tensor_tensor(r_t2[:], r_t2[:], r_t0[:], ALU.add), ("r_t2", "r_t0"), ("r_t2",))
        dv(lambda e: e.reduce_sum(r_c1[:], r_t2[:], axis=mybir.AxisListType.X), ("r_t2",), ("r_c1",))
        dv(lambda e: e.tensor_scalar(r_c1[:], r_c1[:], 1e-8, None, ALU.max), ("r_c1",), ("r_c1",))
        dv(lambda e: e.reciprocal(r_c1[:], r_c1[:]), ("r_c1",), ("r_c1",))
        evw = sc.do("vector", lambda e: e.tensor_scalar(w_dst_ap, r_t2[:], r_c1[:], None, ALU.mult),
                    waits=B("r_t2").rdeps() + B("r_c1").rdeps() + (B(wname).wdeps() if first else []))
        B("r_t2").read(evw)
        B("r_c1").read(evw)
        B(wname).wrote(evw, reset=first)

    B("ones_row").wrote(sc.do("vector", lambda e: e.memset(ones_row[:], 1.0)))
    B("ones_f32").wrote(sc.do("vector", lambda e: e.memset(ones_f32[:], 1.0)))
    B("onesc").wrote(sc.do("vector", lambda e: e.memset(onesc[:], 1.0)))
    B("ksum_bd").wrote(sc.do("vector", lambda e: e.memset(ksum_bd[:], 0.0)))
    B("selb").wrote(dma("gpsimd", selb[:], sel_d))
    B("wrb").wrote(dma("gpsimd", wrb.rearrange("p (dc e) -> p dc e", e=NE),
                       wr_d.rearrange("(dc p) e -> p dc e", p=128)))
    B("bias_k").wrote(dma("gpsimd", bias_k[:], kb_d))
    B("bias_v").wrote(dma("gpsimd", bias_v[:], vb_d))
    B("bias_q").wrote(dma("gpsimd", bias_q[:], qb_d))
    B("bias_o").wrote(dma("gpsimd", bias_o[:], ob_d))
    B("t11").wrote(dma("sync", t11[:], temp_d))
    ev = sc.do("vector", lambda e: e.reciprocal(t11[:], t11[:]), waits=B("t11").rdeps())
    B("t11").wrote(ev, reset=False)
    ev = mm_group([lambda e: e.matmul(ps[7][:, 0:1], ones_f32[:], t11[:], start=True, stop=True)],
                  7, B("t11").rdeps() + B("ones_f32").rdeps())
    ev2 = sc.do("vector", lambda e: e.tensor_copy(invt[:], ps[7][:, 0:1]), waits=bankb[7].rdeps())
    bankb[7].read(ev2)
    B("invt").wrote(ev2)

    def transpose_weight(src_d, dst_d, tag, rcs):
        src3 = src_d.rearrange("(rc p) c -> rc p c", p=128)
        dst3 = dst_d.rearrange("(cc p) r -> p cc r", p=128)
        for rc in rcs:
            stg, stB = wstg[rc % 2], B(f"wstg{rc % 2}")
            ev = dma("gpsimd", stg[:], src3[rc], waits=stB.wdeps())
            stB.wrote(ev)
            tt, ttB = wtt[rc % 2], B(f"wtt{rc % 2}")
            ev = dma("scalar", tt.rearrange("p (cc t) -> p cc t", t=128), stg[:],
                     waits=stB.rdeps() + ttB.wdeps(), transpose=True)
            stB.read(ev)
            ttB.wrote(ev)
            ev = dma("sync", dst3[:, :, rc * 128:(rc + 1) * 128],
                     tt.rearrange("p (cc t) -> p cc t", t=128), waits=ttB.rdeps())
            ttB.read(ev)
            B(f"{tag}{rc}").wrote(ev, reset=False)

    def wT_deps(tag, s):
        w = []
        for rc in range(4 * s, 4 * s + 4):
            w += B(f"{tag}{rc}").rdeps()
        return w

    cast_q = []
    RS = 256

    def _slice(dst, src, tag, e_, r0):
        def emit():
            ev = dma("gpsimd", dst[e_][r0:r0 + RS, :], src[e_][r0:r0 + RS, :])
            B(tag).wrote(ev, reset=False)
        return emit

    for e_ in range(NE):
        for r0 in range(0, D, RS):
            cast_q.append(_slice(wgs_d, wg_d, f"wgs{e_}", e_, r0))
            cast_q.append(_slice(wus_d, wu_d, f"wus{e_}", e_, r0))
            cast_q.append(_slice(wds_d, wd_d, f"wds{e_}", e_, r0))

    def drip(n):
        while n > 0 and cast_q:
            cast_q.pop(0)()
            n -= 1

    x_rows = x_d.rearrange("(n p) d -> n p d", p=128)
    kwT3 = kwT_d.rearrange("(dc p) j -> p dc j", p=128)
    vwT3 = vwT_d.rearrange("(dc p) j -> p dc j", p=128)

    def x_stage(g, drips=True):
        for c in range(4):
            ch = g * 4 + c
            xb, xbB = x_bf[c % 2], B(f"x_bf{c % 2}")
            ev = dma("gpsimd", xb[:], x_rows[ch], waits=xbB.wdeps())
            xbB.wrote(ev)
            if drips:
                drip(3)
            xt, xtB = xT[c], B(f"xT{c}")
            ev = dma("scalar", xt.rearrange("p (dc t) -> p dc t", t=128), xb[:],
                     waits=xbB.rdeps() + xtB.wdeps(), transpose=True)
            xbB.read(ev)
            xtB.wrote(ev)

    x_stage(0, drips=False)
    transpose_weight(kw_d, kwT_d, "kwTs", range(4))
    transpose_weight(vw_d, vwT_d, "vwTs", range(4))
    transpose_weight(kw_d, kwT_d, "kwTs", range(4, NDC))
    transpose_weight(vw_d, vwT_d, "vwTs", range(4, NDC))
    tq = [(lambda rc=rc: transpose_weight(qw_d, qwT_d, "qwTs", [rc])) for rc in range(NDC)]
    tq += [(lambda rc=rc: transpose_weight(ow_d, owT_d, "owTs", [rc])) for rc in range(NDC)]

    def tdrip(n):
        while n > 0 and tq:
            tq.pop(0)()
            n -= 1

    for g in range(NG):
        if g > 0:
            x_stage(g)
            tdrip(5)
            drip(2)
        for buf2, bname, src3, stag, brow, btag, dst4, dtag, is_k in (
                (kwb, "kwb", kwT3, "kwTs", bias_k, "bias_k", kp_t, "kp_t", True),
                (vwb, "vwb", vwT3, "vwTs", bias_v, "bias_v", v_t, "v_t", False)):
            for jb in range(4):
                wb, wbB = buf2[jb % 2], B(f"{bname}{jb % 2}")
                ev = dma("sync", wb.rearrange("p (dc j) -> p dc j", j=512),
                         src3[:, :, jb * 512:(jb + 1) * 512],
                         waits=wbB.wdeps() + wT_deps(stag, jb))
                wbB.wrote(ev)
                wb3 = wb.rearrange("p (dc j) -> p dc j", j=512)
                for c in range(4):
                    xt3 = xT[c].rearrange("p (dc t) -> p dc t", t=128)
                    xtB = B(f"xT{c}")
                    dst, dstB = dst4[c], B(f"{dtag}{c}")
                    bank_i = (jb * 4 + c) % 2
                    p = ps[bank_i]
                    mms = []
                    for dc in range(NDC):
                        mms.append(lambda e, p=p, xt3=xt3, dc=dc, wb3=wb3:
                                   e.matmul(p[:], xt3[:, dc, :], wb3[:, dc, :],
                                            start=(dc == 0), stop=False))
                    mms.append(lambda e, p=p, brow=brow, jb=jb:
                               e.matmul(p[:], ones_row[0:1, 0:128], brow[0:1, jb * 512:(jb + 1) * 512],
                                        start=False, stop=True))
                    ev = mm_group(mms, bank_i, xtB.rdeps() + wbB.rdeps()
                                  + B("ones_row").rdeps() + B(btag).rdeps())
                    xtB.read(ev)
                    wbB.read(ev)
                    first = (jb == 0)
                    if is_k:
                        ev1 = sc.do("scalar", lambda e, p=p: e.activation(out=e_scr[:], in_=p[:],
                                                                          func=AF.Exp),
                                    waits=bankb[bank_i].rdeps() + B("e_scr").wdeps())
                        bankb[bank_i].read(ev1)
                        B("e_scr").wrote(ev1)
                        ev2 = sc.do("vector", lambda e: e.tensor_scalar(e_scr[:], e_scr[:], 1.0,
                                                                        None, ALU.min),
                                    waits=B("e_scr").rdeps())
                        B("e_scr").wrote(ev2, reset=False)
                        ev3 = sc.do("vector", lambda e, p=p, dst=dst, jb=jb:
                                    e.scalar_tensor_tensor(dst[:, jb * 512:(jb + 1) * 512], p[:], 0.0,
                                                           e_scr[:], ALU.max, ALU.add),
                                    waits=bankb[bank_i].rdeps() + B("e_scr").rdeps()
                                    + (dstB.wdeps() if first else []))
                        bankb[bank_i].read(ev3)
                        B("e_scr").read(ev3)
                        dstB.wrote(ev3, reset=first)
                    else:
                        ev3 = sc.do("scalar", lambda e, p=p, dst=dst, jb=jb:
                                    e.activation(out=dst[:, jb * 512:(jb + 1) * 512], in_=p[:],
                                                 func=AF.Copy),
                                    waits=bankb[bank_i].rdeps() + (dstB.wdeps() if first else []))
                        bankb[bank_i].read(ev3)
                        dstB.wrote(ev3, reset=first)
        for h in range(H):
            bi = 3 + (h % 4)
            ev = None
            for c in range(4):
                kpB, vB = B(f"kp_t{c}"), B(f"v_t{c}")
                m = (lambda e, bi=bi, h=h, kp=kp_t[c], v=v_t[c], c=c:
                     e.matmul(ps[bi][:, 0:128], kp[:, h * 128:(h + 1) * 128],
                              v[:, h * 128:(h + 1) * 128], start=(c == 0), stop=(c == 3)))
                w = kpB.rdeps() + vB.rdeps() + (bankb[bi].wdeps() if c == 0 else [])
                sc.do("tensor", m, waits=w)
            for c in range(4):
                kpB = B(f"kp_t{c}")
                m = (lambda e, bi=bi, h=h, kp=kp_t[c], c=c:
                     e.matmul(ps[bi][:, 128:129], kp[:, h * 128:(h + 1) * 128], onesc[:],
                              start=(c == 0), stop=(c == 3)))
                ev = sc.do("tensor", m, waits=kpB.rdeps() + B("onesc").rdeps())
            bankb[bi].wrote(ev, reset=True)
            for c in range(4):
                B(f"kp_t{c}").read(ev)
                B(f"v_t{c}").read(ev)
            if g == 0:
                evk = sc.do("vector", lambda e, bi=bi, h=h:
                            e.tensor_copy(kv_acc[:, h * 128:(h + 1) * 128], ps[bi][:, 0:128]),
                            waits=bankb[bi].rdeps() + (B("kv_acc").wdeps() if h == 0 else []))
                B("kv_acc").wrote(evk, reset=(h == 0))
                evs = sc.do("vector", lambda e, bi=bi, h=h:
                            e.tensor_copy(ksum_acc[:, h:h + 1], ps[bi][:, 128:129]),
                            waits=(B("ksum_acc").wdeps() if h == 0 else []))
                B("ksum_acc").wrote(evs, reset=(h == 0))
            else:
                evk = sc.do("vector", lambda e, bi=bi, h=h:
                            e.tensor_tensor(kv_acc[:, h * 128:(h + 1) * 128],
                                            kv_acc[:, h * 128:(h + 1) * 128],
                                            ps[bi][:, 0:128], ALU.add),
                            waits=bankb[bi].rdeps() + B("kv_acc").rdeps())
                B("kv_acc").wrote(evk, reset=False)
                evs = sc.do("vector", lambda e, bi=bi, h=h:
                            e.tensor_tensor(ksum_acc[:, h:h + 1], ksum_acc[:, h:h + 1],
                                            ps[bi][:, 128:129], ALU.add),
                            waits=B("ksum_acc").rdeps())
                B("ksum_acc").wrote(evs, reset=False)
            bankb[bi].read(evk)
            bankb[bi].read(evs)

    tdrip(len(tq))
    ev = sc.do("vector", lambda e: e.tensor_copy(kvz[:], kv_acc[:]), waits=B("kv_acc").rdeps())
    B("kv_acc").read(ev)
    B("kvz").wrote(ev)
    for h in range(H):
        ev = sc.do("vector", lambda e, h=h: e.tensor_copy(ksum_bd[:, h * 17:h * 17 + 1],
                                                          ksum_acc[:, h:h + 1]),
                   waits=B("ksum_acc").rdeps() + (B("ksum_bd").wdeps() if h == 0 else []))
        B("ksum_acc").read(ev)
        B("ksum_bd").wrote(ev, reset=False)
    if debug:
        ev = dma("sync", dbg_kvz, kv_acc[:], waits=B("kv_acc").rdeps())
        B("outstores").wrote(ev, reset=False)
        ev = dma("sync", dbg_ks, ksum_acc[:], waits=B("ksum_acc").rdeps())
        B("outstores").wrote(ev, reset=False)

    kvz_snap = B("kvz").rdeps()
    bd_snap = B("ksum_bd").rdeps()
    sel_snap = B("selb").rdeps()
    outst_snap = B("outstores").rdeps() if debug else []
    qwT3 = qwT_d.rearrange("(dc p) j -> p dc j", p=128)
    owT3 = owT_d.rearrange("(jc p) i -> p jc i", p=128)
    qwTs_snap = {rc: B(f"qwTs{rc}").rdeps() for rc in range(NDC)}
    owTs_snap = {rc: B(f"owTs{rc}").rdeps() for rc in range(NDC)}
    barrier()
    for nm, snap in (("kvz", kvz_snap), ("ksum_bd", bd_snap), ("selb", sel_snap),
                     ("outstores", outst_snap)):
        for evx in snap:
            B(nm).wrote(evx, reset=False)
    for rc in range(NDC):
        for evx in qwTs_snap[rc]:
            B(f"qwTs{rc}").wrote(evx, reset=False)
        for evx in owTs_snap[rc]:
            B(f"owTs{rc}").wrote(evx, reset=False)

    for ib in range(4):
        wb, wbB = owb[ib], B(f"owb{ib}")
        ev = dma("gpsimd", wb.rearrange("p (jc i) -> p jc i", i=512),
                 owT3[:, :, ib * 512:(ib + 1) * 512],
                 waits=wbB.wdeps() + wT_deps("owTs", ib))
        wbB.wrote(ev)

    qpT3 = qpT.rearrange("p (jc t) -> p jc t", t=512)
    atT3 = attnT.rearrange("p (jc t) -> p jc t", t=512)
    xTfm3 = xT_fm.rearrange("p (dc t) -> p dc t", t=512)
    y_rows = y_d.rearrange("(n p) d -> n p d", p=128)
    for g in range(NG):
        for c in range(4):
            ch = g * 4 + c
            xb, xbB = xB_bf[c % 2], B(f"xB_bf{c % 2}")
            ev = dma("gpsimd", xb[:], x_rows[ch], waits=xbB.wdeps())
            xbB.wrote(ev)
            drip(3)
            ev = dma("scalar", xTfm3[:, :, c * 128:(c + 1) * 128], xb[:],
                     waits=xbB.rdeps() + (B("xT_fm").wdeps() if c == 0 else []), transpose=True)
            xbB.read(ev)
            B("xT_fm").wrote(ev, reset=(c == 0))
        for jb in range(4):
            wb, wbB = qwb[jb % 2], B(f"qwb{jb % 2}")
            ev = dma("sync", wb.rearrange("p (dc j) -> p dc j", j=512),
                     qwT3[:, :, jb * 512:(jb + 1) * 512], waits=wbB.wdeps() + wT_deps("qwTs", jb))
            wbB.wrote(ev)
            wb3 = wb.rearrange("p (dc j) -> p dc j", j=512)
            for jl in range(4):
                jc = jb * 4 + jl
                bank_i = jl % 2
                p = ps[bank_i]
                mms = []
                for dc in range(NDC):
                    mms.append(lambda e, p=p, dc=dc, wb3=wb3, jl=jl:
                               e.matmul(p[:], wb3[:, dc, jl * 128:(jl + 1) * 128], xTfm3[:, dc, :],
                                        start=(dc == 0), stop=False))
                mms.append(lambda e, p=p, jc=jc:
                           e.matmul(p[:], bias_q[0:1, jc * 128:(jc + 1) * 128], ones_row[0:1, 0:512],
                                    start=False, stop=True))
                ev = mm_group(mms, bank_i, B("xT_fm").rdeps() + wbB.rdeps()
                              + B("bias_q").rdeps() + B("ones_row").rdeps())
                B("xT_fm").read(ev)
                wbB.read(ev)
                ev1 = sc.do("scalar", lambda e, p=p: e.activation(out=eB_scr[:], in_=p[:], func=AF.Exp),
                            waits=bankb[bank_i].rdeps() + B("eB_scr").wdeps())
                bankb[bank_i].read(ev1)
                B("eB_scr").wrote(ev1)
                ev2 = sc.do("vector", lambda e: e.tensor_scalar(eB_scr[:], eB_scr[:], 1.0, None, ALU.min),
                            waits=B("eB_scr").rdeps())
                B("eB_scr").wrote(ev2, reset=False)
                ev3 = sc.do("vector", lambda e, p=p, jc=jc:
                            e.scalar_tensor_tensor(qpT3[:, jc, :], p[:], 0.0, eB_scr[:],
                                                   ALU.max, ALU.add),
                            waits=bankb[bank_i].rdeps() + B("eB_scr").rdeps()
                            + (B("qpT").wdeps() if jc == 0 else []))
                bankb[bank_i].read(ev3)
                B("eB_scr").read(ev3)
                B("qpT").wrote(ev3, reset=(jc == 0))
        mms = []
        for jc in range(NDC):
            mms.append(lambda e, jc=jc:
                       e.matmul(ps[2][0:16, :], ksum_bd[:, jc * 16:(jc + 1) * 16], qpT3[:, jc, :],
                                start=(jc == 0), stop=(jc == NDC - 1)))
        ev = mm_group(mms, 2, B("qpT").rdeps() + B("ksum_bd").rdeps())
        B("qpT").read(ev)
        ev1 = sc.do("vector", lambda e: e.tensor_scalar(zs[:], ps[2][0:16, :], ZEPS, None, ALU.add),
                    waits=bankb[2].rdeps() + B("zs").wdeps())
        bankb[2].read(ev1)
        B("zs").wrote(ev1)
        ev2 = sc.do("vector", lambda e: e.reciprocal(zs[:], zs[:]), waits=B("zs").rdeps())
        B("zs").wrote(ev2, reset=False)
        ev3 = sc.do("vector", lambda e: e.tensor_copy(zrec_bf[:], zs[:]),
                    waits=B("zs").rdeps() + B("zrec_bf").wdeps())
        B("zs").read(ev3)
        B("zrec_bf").wrote(ev3)
        for h in range(H):
            ev = mm_group([lambda e, h=h: e.matmul(ps[6][:], selb[:, h * 128:(h + 1) * 128],
                                                   zrec_bf[:], start=True, stop=True)],
                          6, B("selb").rdeps() + B("zrec_bf").rdeps())
            B("zrec_bf").read(ev)
            B("selb").read(ev)
            evq = sc.do("vector", lambda e, h=h: e.tensor_tensor(qpT3[:, h, :], qpT3[:, h, :],
                                                                 ps[6][:], ALU.mult),
                        waits=bankb[6].rdeps() + B("qpT").wdeps())
            bankb[6].read(evq)
            B("qpT").wrote(evq, reset=False)
            bank_i = 4 + h % 2
            ev = mm_group([lambda e, h=h, p=ps[bank_i]:
                           e.matmul(p[:], kvz[:, h * 128:(h + 1) * 128], qpT3[:, h, :],
                                    start=True, stop=True)],
                          bank_i, B("kvz").rdeps() + [evq])
            B("qpT").read(ev)
            B("kvz").read(ev)
            eva = sc.do("scalar", lambda e, h=h, p=ps[bank_i]:
                        e.activation(out=atT3[:, h, :], in_=p[:], func=AF.Copy),
                        waits=bankb[bank_i].rdeps() + (B("attnT").wdeps() if h == 0 else []))
            bankb[bank_i].read(eva)
            B("attnT").wrote(eva, reset=(h == 0))
        for c in range(4):
            ch = g * 4 + c
            evx = dma("gpsimd", x_f32[:], x_rows[ch], waits=B("x_f32").wdeps())
            B("x_f32").wrote(evx)
            for ib in range(4):
                wb, wbB = owb[ib], B(f"owb{ib}")
                wb3 = wb.rearrange("p (jc i) -> p jc i", i=512)
                bank_i = ib % 2
                p = ps[bank_i]
                mms = []
                for jc in range(NDC):
                    mms.append(lambda e, p=p, jc=jc, wb3=wb3, c=c:
                               e.matmul(p[:], atT3[:, jc, c * 128:(c + 1) * 128], wb3[:, jc, :],
                                        start=(jc == 0), stop=False))
                mms.append(lambda e, p=p, ib=ib:
                           e.matmul(p[:], ones_row[0:1, 0:128], bias_o[0:1, ib * 512:(ib + 1) * 512],
                                    start=False, stop=True))
                ev = mm_group(mms, bank_i, B("attnT").rdeps() + wbB.rdeps()
                              + B("bias_o").rdeps() + B("ones_row").rdeps())
                B("attnT").read(ev)
                wbB.read(ev)
                evy = sc.do("vector", lambda e, p=p, ib=ib:
                            e.tensor_tensor(y_f32[:, ib * 512:(ib + 1) * 512], p[:],
                                            x_f32[:, ib * 512:(ib + 1) * 512], ALU.add),
                            waits=bankb[bank_i].rdeps() + B("x_f32").rdeps()
                            + (B("y_f32").wdeps() if ib == 0 else []))
                bankb[bank_i].read(evy)
                B("x_f32").read(evy)
                B("y_f32").wrote(evy, reset=(ib == 0))
            ev1 = sc.do("scalar", lambda e: e.activation(out=x_f32[:], in_=y_f32[:], func=AF.Copy,
                                                         accum_out=sA[:]),
                        waits=B("y_f32").rdeps() + B("x_f32").wdeps() + B("sA").wdeps())
            B("y_f32").read(ev1)
            B("x_f32").wrote(ev1)
            B("sA").wrote(ev1)
            ev2 = sc.do("scalar", lambda e: e.activation(out=x_f32[:], in_=y_f32[:], func=AF.Square,
                                                         accum_out=sB[:]),
                        waits=B("sB").wdeps())
            B("y_f32").read(ev2)
            B("x_f32").wrote(ev2, reset=False)
            B("sB").wrote(ev2)
            ln_apply(y_f32[:], y_f32[:], "y_f32", "y_f32")
            ev = dma("scalar", y_rows[ch], y_f32[:], waits=B("y_f32").rdeps())
            B("y_f32").read(ev)
            B("yscr").wrote(ev, reset=False)
            if debug:
                ev = dma("sync", dbg_y.rearrange("(n p) d -> n p d", p=128)[ch], y_f32[:],
                         waits=B("y_f32").rdeps())
                B("y_f32").read(ev)
                B("outstores").wrote(ev, reset=False)

    drip(len(cast_q))
    y_snap = B("yscr").rdeps()
    wrb_snap = B("wrb").rdeps()
    invt_snap = B("invt").rdeps()
    outst_snap = B("outstores").rdeps() if debug else []
    wsnap = {}
    for e_ in range(NE):
        wsnap[("g", e_)] = B(f"wgs{e_}").rdeps()
        wsnap[("u", e_)] = B(f"wus{e_}").rdeps()
        wsnap[("d", e_)] = B(f"wds{e_}").rdeps()
    wgs3 = [wgs_d[e_].rearrange("(dc p) f -> p dc f", p=128) for e_ in range(NE)]
    wus3 = [wus_d[e_].rearrange("(dc p) f -> p dc f", p=128) for e_ in range(NE)]
    wds3 = [wds_d[e_].rearrange("(fc p) i -> p fc i", p=128) for e_ in range(NE)]
    barrier()
    for evx in y_snap:
        B("yscr").wrote(evx, reset=False)
    for evx in wrb_snap:
        B("wrb").wrote(evx, reset=False)
    for evx in invt_snap:
        B("invt").wrote(evx, reset=False)
    for evx in outst_snap:
        B("outstores").wrote(evx, reset=False)
    for key, snap in wsnap.items():
        kind, e_ = key
        nm = {"g": "wgs", "u": "wus", "d": "wds"}[kind] + str(e_)
        for evx in snap:
            B(nm).wrote(evx, reset=False)

    yT3 = [yT[i].rearrange("p (dc t) -> p dc t", t=512) for i in range(2)]
    oa3 = out_acc.rearrange("p (tc d) -> p tc d", d=D)
    h3 = h_sb.rearrange("p (fc t) -> p fc t", t=512)
    out_rows = out_d.rearrange("(n p) d -> n p d", p=128)

    def oa_init(g):
        for c in range(4):
            ch = g * 4 + c
            ev = dma("gpsimd", oa3[:, c, :], y_rows[ch],
                     waits=B("out_acc").wdeps() + B("yscr").rdeps())
            B("out_acc").wrote(ev, reset=(c == 0))

    def yt_stage(g):
        par = g % 2
        for c in range(4):
            ch = g * 4 + c
            ybB = B("y_bf")
            ev = dma("gpsimd", y_bf[:], y_rows[ch], waits=ybB.wdeps() + B("yscr").rdeps())
            ybB.wrote(ev)
            ev = dma("scalar", yT3[par][:, :, c * 128:(c + 1) * 128], y_bf[:],
                     waits=ybB.rdeps() + (B(f"yT{par}").wdeps() if c == 0 else []),
                     transpose=True)
            ybB.read(ev)
            B(f"yT{par}").wrote(ev, reset=(c == 0))

    def router_chunk(g, c):
        par = g % 2
        mms = []
        for dc in range(NDC):
            mms.append(lambda e, dc=dc, c=c, par=par:
                       e.matmul(ps[2][:, 0:NE], yT3[par][:, dc, c * 128:(c + 1) * 128],
                                wrb[:, dc * NE:(dc + 1) * NE],
                                start=(dc == 0), stop=(dc == NDC - 1)))
        ev = mm_group(mms, 2, B(f"yT{par}").rdeps() + B("wrb").rdeps())
        B(f"yT{par}").read(ev)
        router(ps[2][:, 0:NE], w_all[par][:, c * NE:(c + 1) * NE], c == 0, f"w_all{par}")

    oa_init(0)
    yt_stage(0)
    for c in range(4):
        router_chunk(0, c)
    for g in range(NG):
        par = g % 2
        if g + 1 < NG:
            yt_stage(g + 1)
        for e_ in range(NE):
            if e_ >= 4 and g + 1 < NG:
                router_chunk(g + 1, e_ - 4)
            for fb in range(4):
                gb, gbB = wgb[fb % 2], B(f"wgb{fb % 2}")
                ev = dma("sync", gb.rearrange("p (dc f) -> p dc f", f=512),
                         wgs3[e_][:, :, fb * 512:(fb + 1) * 512],
                         waits=gbB.wdeps() + B(f"wgs{e_}").rdeps())
                gbB.wrote(ev)
                ub, ubB = wub[fb % 2], B(f"wub{fb % 2}")
                ev = dma("sync", ub.rearrange("p (dc f) -> p dc f", f=512),
                         wus3[e_][:, :, fb * 512:(fb + 1) * 512],
                         waits=ubB.wdeps() + B(f"wus{e_}").rdeps())
                ubB.wrote(ev)
                gb3 = gb.rearrange("p (dc f) -> p dc f", f=512)
                ub3 = ub.rearrange("p (dc f) -> p dc f", f=512)
                for jf in range(4):
                    fg = fb * 4 + jf
                    mmsg = []
                    mmsu = []
                    for dc in range(NDC):
                        mmsg.append(lambda e, dc=dc, jf=jf, gb3=gb3, par=par:
                                    e.matmul(ps[0][:], gb3[:, dc, jf * 128:(jf + 1) * 128],
                                             yT3[par][:, dc, :], start=(dc == 0), stop=(dc == NDC - 1)))
                        mmsu.append(lambda e, dc=dc, jf=jf, ub3=ub3, par=par:
                                    e.matmul(ps[1][:], ub3[:, dc, jf * 128:(jf + 1) * 128],
                                             yT3[par][:, dc, :], start=(dc == 0), stop=(dc == NDC - 1)))
                    evg = mm_group(mmsg, 0, B(f"yT{par}").rdeps() + gbB.rdeps())
                    evu = mm_group(mmsu, 1, B(f"yT{par}").rdeps() + ubB.rdeps())
                    B(f"yT{par}").read(evu)
                    gbB.read(evg)
                    ubB.read(evu)
                    evs = sc.do("scalar", lambda e: e.activation(out=s_sb[:], in_=ps[0][:], func=AF.Silu),
                                waits=bankb[0].rdeps() + B("s_sb").wdeps())
                    bankb[0].read(evs)
                    B("s_sb").wrote(evs)
                    evh = sc.do("vector", lambda e, fg=fg: e.tensor_tensor(h3[:, fg, :], ps[1][:],
                                                                           s_sb[:], ALU.mult),
                                waits=bankb[1].rdeps() + B("s_sb").rdeps()
                                + (B("h_sb").wdeps() if fg == 0 else []))
                    bankb[1].read(evh)
                    B("s_sb").read(evh)
                    B("h_sb").wrote(evh, reset=(fg == 0))
            for ib in range(4):
                db, dbB = wdb[ib % 2], B(f"wdb{ib % 2}")
                ev = dma("sync", db.rearrange("p (fc i) -> p fc i", i=512),
                         wds3[e_][:, :, ib * 512:(ib + 1) * 512],
                         waits=dbB.wdeps() + B(f"wds{e_}").rdeps())
                dbB.wrote(ev)
                db3 = db.rearrange("p (fc i) -> p fc i", i=512)
                for tc in range(4):
                    bank_i = 3 + tc
                    p = ps[bank_i]
                    mms = []
                    for fc in range(NDC):
                        mms.append(lambda e, p=p, fc=fc, tc=tc, db3=db3:
                                   e.matmul(p[:], h3[:, fc, tc * 128:(tc + 1) * 128], db3[:, fc, :],
                                            start=(fc == 0), stop=(fc == NDC - 1)))
                    ev = mm_group(mms, bank_i, B("h_sb").rdeps() + dbB.rdeps())
                    B("h_sb").read(ev)
                    dbB.read(ev)
                    evo = sc.do("vector", lambda e, p=p, tc=tc, ib=ib, e_=e_, par=par:
                                e.scalar_tensor_tensor(oa3[:, tc, ib * 512:(ib + 1) * 512], p[:],
                                                       w_all[par][:, tc * NE + e_:tc * NE + e_ + 1],
                                                       oa3[:, tc, ib * 512:(ib + 1) * 512],
                                                       ALU.mult, ALU.add),
                                waits=bankb[bank_i].rdeps() + B(f"w_all{par}").rdeps()
                                + B("out_acc").rdeps())
                    bankb[bank_i].read(evo)
                    B(f"w_all{par}").read(evo)
                    B("out_acc").wrote(evo, reset=False)
        for c in range(4):
            ch = g * 4 + c
            ev1 = sc.do("scalar", lambda e, c=c: e.activation(out=o_f32[:], in_=oa3[:, c, :],
                                                              func=AF.Copy, accum_out=sA[:]),
                        waits=B("out_acc").rdeps() + B("o_f32").wdeps() + B("sA").wdeps())
            B("out_acc").read(ev1)
            B("o_f32").wrote(ev1)
            B("sA").wrote(ev1)
            ev2 = sc.do("scalar", lambda e, c=c: e.activation(out=o_f32[:], in_=oa3[:, c, :],
                                                              func=AF.Square, accum_out=sB[:]),
                        waits=B("sB").wdeps())
            B("out_acc").read(ev2)
            B("o_f32").wrote(ev2, reset=False)
            B("sB").wrote(ev2)
            ln_apply(oa3[:, c, :], o_f32[:], "out_acc", "o_f32")
            ev = dma("scalar", out_rows[ch], o_f32[:], waits=B("o_f32").rdeps())
            B("o_f32").read(ev)
            B("outstores").wrote(ev, reset=False)
        if g + 1 < NG:
            oa_init(g + 1)

    final = B("outstores").rdeps()
    sc.steps["sync"].append((sorted(dict(final).items()), None, None, 0))

    from contextlib import ExitStack
    es = ExitStack()
    sems = {n: es.enter_context(nc.semaphore(n)) for n in sorted(sc.cnt.keys())}
    block = es.enter_context(nc.Block())

    def mk_body(engname):
        def body(eng):
            for ws, emit, sem, amt in sc.steps[engname]:
                for s, v in ws:
                    eng.wait_ge(sems[s], v)
                if emit is not None:
                    inst = emit(eng)
                    if sem is not None:
                        inst.then_inc(sems[sem], amt)
        return body

    for engname in ENGINES:
        getattr(block, engname)(mk_body(engname))
    es.close()
    return nc


_NC_CACHE = {}


def _sel_mat():
    return np.kron(np.eye(16, dtype=np.float32), np.ones((1, 128), np.float32))


def kernel(L0_attn_hidden_states, L0_attn_q_w, L0_attn_q_b, L0_attn_k_w, L0_attn_k_b,
           L0_attn_v_w, L0_attn_v_b, L0_attn_o_w, L0_attn_o_b,
           L0_ffn_Wg, L0_ffn_Wu, L0_ffn_Wd, L0_ffn_Wr, L0_ffn_temperature):
    x = np.ascontiguousarray(np.asarray(L0_attn_hidden_states, dtype=np.float32))
    Bn, S, Dm = x.shape
    assert Dm == D
    key = S
    if key not in _NC_CACHE:
        _NC_CACHE[key] = build_program(S)
    nc = _NC_CACHE[key]
    common = {
        "qw": np.ascontiguousarray(np.asarray(L0_attn_q_w, dtype=np.float32)),
        "kw": np.ascontiguousarray(np.asarray(L0_attn_k_w, dtype=np.float32)),
        "vw": np.ascontiguousarray(np.asarray(L0_attn_v_w, dtype=np.float32)),
        "ow": np.ascontiguousarray(np.asarray(L0_attn_o_w, dtype=np.float32)),
        "qb": np.ascontiguousarray(np.asarray(L0_attn_q_b, dtype=np.float32).reshape(1, D)),
        "kb": np.ascontiguousarray(np.asarray(L0_attn_k_b, dtype=np.float32).reshape(1, D)),
        "vb": np.ascontiguousarray(np.asarray(L0_attn_v_b, dtype=np.float32).reshape(1, D)),
        "ob": np.ascontiguousarray(np.asarray(L0_attn_o_b, dtype=np.float32).reshape(1, D)),
        "wg": np.ascontiguousarray(np.asarray(L0_ffn_Wg, dtype=np.float32)),
        "wu": np.ascontiguousarray(np.asarray(L0_ffn_Wu, dtype=np.float32)),
        "wd": np.ascontiguousarray(np.asarray(L0_ffn_Wd, dtype=np.float32)),
        "wr": np.ascontiguousarray(np.asarray(L0_ffn_Wr, dtype=np.float32)),
        "temp": np.ascontiguousarray(np.asarray(L0_ffn_temperature, dtype=np.float32).reshape(1, 1)),
        "sel": _sel_mat(),
    }
    in_maps = [dict(common, x=np.ascontiguousarray(x[b])) for b in range(Bn)]
    res = run_bass_kernel_spmd(nc, in_maps, core_ids=list(range(Bn)))
    return np.stack([np.asarray(r["out"], dtype=np.float32) for r in res.results], axis=0)


# version 05 (final)
# speedup vs baseline: 1.0683x; 1.0683x over previous
import numpy as np
import concourse.bass as bass
import concourse.mybir as mybir
from concourse.bass_utils import run_bass_kernel_spmd

F32 = mybir.dt.float32
BF16 = mybir.dt.bfloat16
AF = mybir.ActivationFunctionType
ALU = mybir.AluOpType

D = 2048
NDC = 16
H = 16
NE = 8
GL = 0.5 - float(np.log(4.0 / 3.0))
GC = float(1.0 / np.e)
GU = 0.5
ZEPS = 1e-6
LNEPS = 1e-5

ENGINES = ("sync", "scalar", "vector", "tensor", "gpsimd")
QSEM = {"sync": "sy", "scalar": "sc", "gpsimd": "gq"}
ESEM = {"tensor": "pe", "scalar": "ac", "vector": "dv", "gpsimd": "gp", "sync": "sq"}


class Sch:
    def __init__(self):
        self.steps = {e: [] for e in ENGINES}
        self.cnt = {}
        self.marks = {e: {} for e in ENGINES}

    def do(self, eng, emit, waits=(), dma=False):
        ws = {}
        for w in waits:
            if not w:
                continue
            s, v = w
            if v <= 0:
                continue
            if eng == "tensor" and s == "pe":
                continue
            if self.marks[eng].get(s, 0) >= v:
                continue
            ws[s] = max(ws.get(s, 0), v)
        for s, v in ws.items():
            self.marks[eng][s] = v
        sem = QSEM[eng] if dma else ESEM[eng]
        amt = 16 if dma else 1
        self.cnt[sem] = self.cnt.get(sem, 0) + amt
        ev = (sem, self.cnt[sem])
        self.steps[eng].append((sorted(ws.items()), emit, sem, amt))
        return ev

    def raw(self, eng, emit):
        self.steps[eng].append(((), emit, None, 0))

    def snapshot(self):
        return list(self.cnt.items())


class Buf:
    def __init__(self, seed=()):
        self.w = {}
        self.r = {}
        for s, v in seed:
            self.r[s] = max(self.r.get(s, 0), v)

    def rdeps(self):
        return list(self.w.items())

    def wdeps(self):
        d = dict(self.w)
        for s, v in self.r.items():
            d[s] = max(d.get(s, 0), v)
        return list(d.items())

    def wrote(self, ev, reset=True):
        if reset:
            self.w = {}
            self.r = {}
        if ev:
            s, v = ev
            self.w[s] = max(self.w.get(s, 0), v)

    def read(self, ev):
        if ev:
            s, v = ev
            self.r[s] = max(self.r.get(s, 0), v)


def build_program(S=4096, debug=False):
    assert S % 512 == 0
    NG = S // 512

    nc = bass.Bass("TRN2", target_bir_lowering=False)

    x_d = nc.dram_tensor("x", [S, D], F32, kind="ExternalInput").ap()
    qw_d = nc.dram_tensor("qw", [D, D], F32, kind="ExternalInput").ap()
    kw_d = nc.dram_tensor("kw", [D, D], F32, kind="ExternalInput").ap()
    vw_d = nc.dram_tensor("vw", [D, D], F32, kind="ExternalInput").ap()
    ow_d = nc.dram_tensor("ow", [D, D], F32, kind="ExternalInput").ap()
    qb_d = nc.dram_tensor("qb", [1, D], F32, kind="ExternalInput").ap()
    kb_d = nc.dram_tensor("kb", [1, D], F32, kind="ExternalInput").ap()
    vb_d = nc.dram_tensor("vb", [1, D], F32, kind="ExternalInput").ap()
    ob_d = nc.dram_tensor("ob", [1, D], F32, kind="ExternalInput").ap()
    wg_d = nc.dram_tensor("wg", [NE, D, D], F32, kind="ExternalInput").ap()
    wu_d = nc.dram_tensor("wu", [NE, D, D], F32, kind="ExternalInput").ap()
    wd_d = nc.dram_tensor("wd", [NE, D, D], F32, kind="ExternalInput").ap()
    wr_d = nc.dram_tensor("wr", [D, NE], F32, kind="ExternalInput").ap()
    temp_d = nc.dram_tensor("temp", [1, 1], F32, kind="ExternalInput").ap()
    sel_d = nc.dram_tensor("sel", [16, D], F32, kind="ExternalInput").ap()
    out_d = nc.dram_tensor("out", [S, D], F32, kind="ExternalOutput").ap()
    y_d = nc.dram_tensor("yscr", [S, D], F32, kind="Internal").ap()
    qwT_d = nc.dram_tensor("qwTs", [D, D], BF16, kind="Internal").ap()
    kwT_d = nc.dram_tensor("kwTs", [D, D], BF16, kind="Internal").ap()
    vwT_d = nc.dram_tensor("vwTs", [D, D], BF16, kind="Internal").ap()
    owT_d = nc.dram_tensor("owTs", [D, D], BF16, kind="Internal").ap()
    wgs_d = nc.dram_tensor("wgs", [NE, D, D], BF16, kind="Internal").ap()
    wus_d = nc.dram_tensor("wus", [NE, D, D], BF16, kind="Internal").ap()
    wds_d = nc.dram_tensor("wds", [NE, D, D], BF16, kind="Internal").ap()
    if debug:
        dbg_kvz = nc.dram_tensor("dbg_kvz", [128, D], F32, kind="ExternalOutput").ap()
        dbg_ks = nc.dram_tensor("dbg_ks", [128, 16], F32, kind="ExternalOutput").ap()
        dbg_y = nc.dram_tensor("dbg_y", [S, D], F32, kind="ExternalOutput").ap()

    cur = [20480]
    hwm = [20480]
    offs = {}

    def place(name, shape, dtype, at=None):
        nbytes = int(np.prod(shape[1:])) * (2 if dtype == BF16 else 4)
        if at is None:
            off = (cur[0] + 63) & ~63
            cur[0] = off + nbytes
        else:
            off = at
        offs[name] = off
        hwm[0] = max(hwm[0], off + nbytes)
        return nc.alloc_sbuf_tensor_at(name, list(shape), dtype, offset=off).ap()

    ones_row = place("ones_row", [1, 512], BF16)
    ones_f32 = place("ones_f32", [1, 128], F32)
    onesc = place("onesc", [128, 1], BF16)
    bias_k = place("bias_k", [1, D], BF16)
    bias_v = place("bias_v", [1, D], BF16)
    bias_q = place("bias_q", [1, D], BF16)
    bias_o = place("bias_o", [1, D], BF16)
    wrb = place("wrb", [128, NDC * NE], BF16)
    invt = place("invt", [128, 1], F32)
    t11 = place("t11", [1, 1], F32)
    selb = place("selb", [16, D], BF16)
    ksum_bd = place("ksum_bd", [128, 256], BF16)
    kvz = place("kvz", [128, D], BF16)
    ksum_acc = place("ksum_acc", [128, 16], F32)
    sA = place("sA", [128, 1], F32)
    sB = place("sB", [128, 1], F32)
    st0 = place("st0", [128, 1], F32)
    st1 = place("st1", [128, 1], F32)
    st2 = place("st2", [128, 1], F32)
    rstd = place("rstd", [128, 1], F32)
    negmr = place("negmr", [128, 1], F32)
    r_t0 = place("r_t0", [128, NE], F32)
    r_t1 = place("r_t1", [128, NE], F32)
    r_t2 = place("r_t2", [128, NE], F32)
    r_inh = place("r_inh", [128, NE], F32)
    r_c0 = place("r_c0", [128, 1], F32)
    r_c1 = place("r_c1", [128, 1], F32)
    w_all = [place(f"w_all{i}", [128, 4 * NE], F32) for i in range(2)]

    STREAM = (cur[0] + 63) & ~63
    kwb = [place(f"kwb{i}", [128, NDC * 512], BF16, STREAM + i * 16384) for i in range(2)]
    vwb = [place(f"vwb{i}", [128, NDC * 512], BF16, STREAM + (2 + i) * 16384) for i in range(2)]
    qwb = [place(f"qwb{i}", [128, NDC * 512], BF16, STREAM + i * 16384) for i in range(2)]
    owb = [place(f"owb{i}", [128, NDC * 512], BF16, STREAM + (2 + i) * 16384) for i in range(4)]
    wgb = [place(f"wgb{i}", [128, NDC * 512], BF16, STREAM + i * 16384) for i in range(2)]
    wub = [place(f"wub{i}", [128, NDC * 512], BF16, STREAM + (2 + i) * 16384) for i in range(2)]
    wdb = [place(f"wdb{i}", [128, NDC * 512], BF16, STREAM + (4 + i) * 16384) for i in range(2)]
    WORK = STREAM + 6 * 16384

    wstg = [place(f"wstg{i}", [128, D], BF16, WORK + i * 4096) for i in range(2)]
    wtt = [place(f"wtt{i}", [128, NDC * 128], BF16, WORK + 8192 + i * 4096) for i in range(2)]
    x_bf = [place(f"x_bf{i}", [128, D], BF16, WORK + 16384 + i * 4096) for i in range(2)]
    xT = [place(f"xT{i}", [128, NDC * 128], BF16, WORK + 24576 + i * 4096) for i in range(4)]
    kp_t = [place(f"kp_t{i}", [128, D], BF16, WORK + 40960 + i * 4096) for i in range(4)]
    v_t = [place(f"v_t{i}", [128, D], BF16, WORK + 57344 + i * 4096) for i in range(4)]
    e_scr = place("e_scr", [128, 512], F32, STREAM + 4 * 16384)
    kv_acc = place("kv_acc", [128, D], F32, STREAM + 4 * 16384 + 2048)
    xB_bf = [place(f"xB_bf{i}", [128, D], BF16, WORK + i * 4096) for i in range(2)]
    xT_fm = place("xT_fm", [128, NDC * 512], BF16, WORK + 8192)
    qpT = place("qpT", [128, NDC * 512], BF16, WORK + 24576)
    attnT = place("attnT", [128, NDC * 512], BF16, WORK + 40960)
    eB_scr = place("eB_scr", [128, 512], F32, WORK + 57344)
    zs = place("zs", [16, 512], F32, WORK + 59392)
    zrec_bf = place("zrec_bf", [16, 512], BF16, WORK + 61440)
    x_f32 = place("x_f32", [128, D], F32, WORK + 62464)
    y_f32 = place("y_f32", [128, D], F32, WORK + 70656)
    assert offs["bias_v"] == offs["bias_k"] + 4096 and offs["bias_o"] == offs["bias_k"] + 12288
    yT = [place("yT0", [128, NDC * 512], BF16, WORK),
          place("yT1", [128, NDC * 512], BF16, offs["bias_k"])]
    y_bf = place("y_bf", [128, D], BF16, WORK + 16384)
    out_acc = place("out_acc", [128, 4 * D], F32, WORK + 20480)
    h_sb = place("h_sb", [128, 16 * 512], BF16, WORK + 53248)
    s_sb = place("s_sb", [128, 512], BF16, WORK + 69632)
    o_f32 = place("o_f32", [128, D], F32, WORK + 70656)
    assert max(hwm[0], WORK + 78848) <= 229376, (hwm[0], WORK)

    ps = [nc.alloc_psum_tensor(f"ps{i}", [128, 512], F32).ap() for i in range(8)]

    sc = Sch()
    bankb = [Buf() for _ in range(8)]
    bufs = {}
    cur_seed = [()]

    def B(name):
        if name not in bufs:
            bufs[name] = Buf(cur_seed[0])
        return bufs[name]

    def barrier():
        snap = sc.snapshot()
        cur_seed[0] = snap
        bufs.clear()
        for i in range(8):
            bankb[i] = Buf(snap)

    def dma(queue, out, in_, waits=(), transpose=False):
        if transpose:
            return sc.do(queue, lambda e, o=out, i=in_: e.dma_start(out=o, in_=i, transpose=True),
                         waits=waits, dma=True)
        return sc.do(queue, lambda e, o=out, i=in_: e.dma_start(out=o, in_=i),
                     waits=waits, dma=True)

    def mm_group(mms, bank_i, extra_waits=()):
        bb = bankb[bank_i]
        w = bb.wdeps() + list(extra_waits)
        ev = None
        for i, m in enumerate(mms):
            if len(mms) == 1:
                ev = sc.do("tensor", m, waits=w)
            elif i == 0:
                sc.do("tensor", m, waits=w)
            elif i == len(mms) - 1:
                ev = sc.do("tensor", m)
            else:
                sc.raw("tensor", m)
        bb.wrote(ev, reset=True)
        return ev

    def ln_apply(src_ap, dst_ap, src_buf, dst_buf):
        w0 = B("sA").rdeps() + B("sB").rdeps() + B("st0").wdeps() + B("st1").wdeps() + B("st2").wdeps()
        ev = sc.do("vector", lambda e: e.tensor_scalar(st0[:], sA[:], 1.0 / D, None, ALU.mult), waits=w0)
        B("sA").read(ev)
        B("st0").wrote(ev)
        ev = sc.do("vector", lambda e: e.tensor_scalar(st1[:], sB[:], 1.0 / D, None, ALU.mult),
                   waits=B("st1").wdeps())
        B("sB").read(ev)
        B("st1").wrote(ev)
        ev = sc.do("vector", lambda e: e.tensor_tensor(st2[:], st0[:], st0[:], ALU.mult),
                   waits=B("st0").rdeps() + B("st2").wdeps())
        B("st2").wrote(ev)
        ev = sc.do("vector", lambda e: e.tensor_tensor(st1[:], st1[:], st2[:], ALU.subtract),
                   waits=B("st1").rdeps() + B("st2").rdeps())
        B("st2").read(ev)
        B("st1").wrote(ev, reset=False)
        ev = sc.do("vector", lambda e: e.tensor_scalar(st1[:], st1[:], LNEPS, None, ALU.add),
                   waits=B("st1").rdeps())
        B("st1").wrote(ev, reset=False)
        eva = sc.do("scalar", lambda e: e.activation(out=st2[:], in_=st1[:], func=AF.Sqrt),
                    waits=B("st1").rdeps() + B("st2").wdeps())
        B("st1").read(eva)
        B("st2").wrote(eva)
        evr = sc.do("vector", lambda e: e.reciprocal(rstd[:], st2[:]),
                    waits=B("st2").rdeps() + B("rstd").wdeps())
        B("st2").read(evr)
        B("rstd").wrote(evr)
        evn = sc.do("vector", lambda e: e.tensor_scalar(negmr[:], st0[:], rstd[:], -1.0, ALU.mult, ALU.mult),
                    waits=B("st0").rdeps() + B("rstd").rdeps() + B("negmr").wdeps())
        B("st0").read(evn)
        B("rstd").read(evn)
        B("negmr").wrote(evn)
        evo = sc.do("vector", lambda e: e.tensor_scalar(dst_ap, src_ap, rstd[:], negmr[:], ALU.mult, ALU.add),
                    waits=B("rstd").rdeps() + B("negmr").rdeps() + B(src_buf).rdeps()
                    + (B(dst_buf).wdeps() if dst_buf != src_buf else []))
        B("rstd").read(evo)
        B("negmr").read(evo)
        B(src_buf).read(evo)
        B(dst_buf).wrote(evo, reset=(dst_buf != src_buf))
        return evo

    def router(psum_ap, w_dst_ap, first, wname="w_all"):
        ev = sc.do("scalar", lambda e: e.activation(out=r_inh[:], in_=psum_ap, func=AF.Sigmoid,
                                                    scale=invt[:]),
                   waits=bankb[2].rdeps() + B("invt").rdeps() + B("r_inh").wdeps())
        bankb[2].read(ev)
        B("r_inh").wrote(ev)

        def dv(emit, reads=(), writes=()):
            w = []
            for nm in reads:
                w += B(nm).rdeps()
            for nm in writes:
                w += B(nm).wdeps()
            ev = sc.do("vector", emit, waits=w)
            for nm in reads:
                B(nm).read(ev)
            for nm in writes:
                B(nm).wrote(ev)
            return ev

        dv(lambda e: e.tensor_scalar(r_t0[:], r_inh[:], GL, None, ALU.is_ge), ("r_inh",), ("r_t0",))
        dv(lambda e: e.tensor_scalar(r_t1[:], r_inh[:], GU, None, ALU.is_le), ("r_inh",), ("r_t1",))
        dv(lambda e: e.tensor_tensor(r_t0[:], r_t0[:], r_t1[:], ALU.mult), ("r_t0", "r_t1"), ("r_t0",))
        dv(lambda e: e.tensor_scalar(r_t1[:], r_inh[:], GC, None, ALU.subtract), ("r_inh",), ("r_t1",))
        dv(lambda e: e.tensor_scalar(r_t2[:], r_t1[:], -1.0, None, ALU.mult), ("r_t1",), ("r_t2",))
        dv(lambda e: e.tensor_tensor(r_t1[:], r_t1[:], r_t2[:], ALU.max), ("r_t1", "r_t2"), ("r_t1",))
        evd = dv(lambda e: e.tensor_scalar(r_t2[:], r_t1[:], -10.0, None, ALU.mult), ("r_t1",), ("r_t2",))
        evd2 = dv(lambda e: e.tensor_scalar(r_t1[:], r_t1[:], -(1.0 / 0.3), None, ALU.mult),
                  ("r_t1",), ("r_t1",))
        eva = sc.do("scalar", lambda e: e.activation(out=r_t2[:], in_=r_t2[:], func=AF.Exp),
                    waits=B("r_t2").rdeps())
        B("r_t2").wrote(eva, reset=False)
        evb = sc.do("scalar", lambda e: e.activation(out=r_inh[:], in_=r_t1[:], func=AF.Exp),
                    waits=B("r_t1").rdeps() + B("r_inh").wdeps())
        B("r_t1").read(evb)
        B("r_inh").wrote(evb)
        dv(lambda e: e.tensor_tensor(r_t2[:], r_t2[:], r_t0[:], ALU.mult), ("r_t2", "r_t0"), ("r_t2",))
        dv(lambda e: e.reduce_sum(r_c0[:], r_t2[:], axis=mybir.AxisListType.X), ("r_t2",), ("r_c0",))
        dv(lambda e: e.reduce_max(r_c1[:], r_inh[:], axis=mybir.AxisListType.X), ("r_inh",), ("r_c1",))
        dv(lambda e: e.tensor_scalar(r_t0[:], r_inh[:], r_c1[:], None, ALU.is_ge),
           ("r_inh", "r_c1"), ("r_t0",))
        dv(lambda e: e.scalar_tensor_tensor(r_t1[:], r_t0[:], -1e9, r_inh[:], ALU.mult, ALU.add),
           ("r_t0", "r_inh"), ("r_t1",))
        dv(lambda e: e.reduce_max(r_c1[:], r_t1[:], axis=mybir.AxisListType.X), ("r_t1",), ("r_c1",))
        dv(lambda e: e.tensor_scalar(r_t1[:], r_t1[:], r_c1[:], None, ALU.is_ge),
           ("r_t1", "r_c1"), ("r_t1",))
        dv(lambda e: e.tensor_tensor(r_t0[:], r_t0[:], r_t1[:], ALU.add), ("r_t0", "r_t1"), ("r_t0",))
        dv(lambda e: e.tensor_tensor(r_t0[:], r_t0[:], r_inh[:], ALU.mult), ("r_t0", "r_inh"), ("r_t0",))
        dv(lambda e: e.reduce_sum(r_c1[:], r_t0[:], axis=mybir.AxisListType.X), ("r_t0",), ("r_c1",))
        dv(lambda e: e.tensor_scalar(r_c1[:], r_c1[:], 1e-8, None, ALU.max), ("r_c1",), ("r_c1",))
        dv(lambda e: e.reciprocal(r_c1[:], r_c1[:]), ("r_c1",), ("r_c1",))
        dv(lambda e: e.tensor_scalar(r_t0[:], r_t0[:], r_c1[:], None, ALU.mult), ("r_t0", "r_c1"), ("r_t0",))
        dv(lambda e: e.tensor_scalar(r_c1[:], r_c0[:], 1e-8, None, ALU.is_lt), ("r_c0",), ("r_c1",))
        dv(lambda e: e.tensor_tensor(r_t0[:], r_t0[:], r_t2[:], ALU.subtract), ("r_t0", "r_t2"), ("r_t0",))
        dv(lambda e: e.tensor_scalar(r_t0[:], r_t0[:], r_c1[:], None, ALU.mult), ("r_t0", "r_c1"), ("r_t0",))
        dv(lambda e: e.tensor_tensor(r_t2[:], r_t2[:], r_t0[:], ALU.add), ("r_t2", "r_t0"), ("r_t2",))
        dv(lambda e: e.reduce_sum(r_c1[:], r_t2[:], axis=mybir.AxisListType.X), ("r_t2",), ("r_c1",))
        dv(lambda e: e.tensor_scalar(r_c1[:], r_c1[:], 1e-8, None, ALU.max), ("r_c1",), ("r_c1",))
        dv(lambda e: e.reciprocal(r_c1[:], r_c1[:]), ("r_c1",), ("r_c1",))
        evw = sc.do("vector", lambda e: e.tensor_scalar(w_dst_ap, r_t2[:], r_c1[:], None, ALU.mult),
                    waits=B("r_t2").rdeps() + B("r_c1").rdeps() + (B(wname).wdeps() if first else []))
        B("r_t2").read(evw)
        B("r_c1").read(evw)
        B(wname).wrote(evw, reset=first)

    B("ones_row").wrote(sc.do("vector", lambda e: e.memset(ones_row[:], 1.0)))
    B("ones_f32").wrote(sc.do("vector", lambda e: e.memset(ones_f32[:], 1.0)))
    B("onesc").wrote(sc.do("vector", lambda e: e.memset(onesc[:], 1.0)))
    B("ksum_bd").wrote(sc.do("vector", lambda e: e.memset(ksum_bd[:], 0.0)))
    B("selb").wrote(dma("gpsimd", selb[:], sel_d))
    B("wrb").wrote(dma("gpsimd", wrb.rearrange("p (dc e) -> p dc e", e=NE),
                       wr_d.rearrange("(dc p) e -> p dc e", p=128)))
    B("bias_k").wrote(dma("gpsimd", bias_k[:], kb_d))
    B("bias_v").wrote(dma("gpsimd", bias_v[:], vb_d))
    B("bias_q").wrote(dma("gpsimd", bias_q[:], qb_d))
    B("bias_o").wrote(dma("gpsimd", bias_o[:], ob_d))
    B("t11").wrote(dma("sync", t11[:], temp_d))
    ev = sc.do("vector", lambda e: e.reciprocal(t11[:], t11[:]), waits=B("t11").rdeps())
    B("t11").wrote(ev, reset=False)
    ev = mm_group([lambda e: e.matmul(ps[7][:, 0:1], ones_f32[:], t11[:], start=True, stop=True)],
                  7, B("t11").rdeps() + B("ones_f32").rdeps())
    ev2 = sc.do("vector", lambda e: e.tensor_copy(invt[:], ps[7][:, 0:1]), waits=bankb[7].rdeps())
    bankb[7].read(ev2)
    B("invt").wrote(ev2)

    def transpose_weight(src_d, dst_d, tag, rcs):
        src3 = src_d.rearrange("(rc p) c -> rc p c", p=128)
        dst3 = dst_d.rearrange("(cc p) r -> p cc r", p=128)
        for rc in rcs:
            stg, stB = wstg[rc % 2], B(f"wstg{rc % 2}")
            ev = dma("gpsimd", stg[:], src3[rc], waits=stB.wdeps())
            stB.wrote(ev)
            tt, ttB = wtt[rc % 2], B(f"wtt{rc % 2}")
            ev = dma("scalar", tt.rearrange("p (cc t) -> p cc t", t=128), stg[:],
                     waits=stB.rdeps() + ttB.wdeps(), transpose=True)
            stB.read(ev)
            ttB.wrote(ev)
            ev = dma("sync", dst3[:, :, rc * 128:(rc + 1) * 128],
                     tt.rearrange("p (cc t) -> p cc t", t=128), waits=ttB.rdeps())
            ttB.read(ev)
            B(f"{tag}{rc}").wrote(ev, reset=False)

    def wT_deps(tag, s):
        w = []
        for rc in range(4 * s, 4 * s + 4):
            w += B(f"{tag}{rc}").rdeps()
        return w

    cast_q = []
    RS = 256

    def _slice(dst, src, tag, e_, r0):
        def emit():
            ev = dma("gpsimd", dst[e_][r0:r0 + RS, :], src[e_][r0:r0 + RS, :])
            B(tag).wrote(ev, reset=False)
        return emit

    for e_ in range(NE):
        for r0 in range(0, D, RS):
            cast_q.append(_slice(wgs_d, wg_d, f"wgs{e_}", e_, r0))
            cast_q.append(_slice(wus_d, wu_d, f"wus{e_}", e_, r0))
            cast_q.append(_slice(wds_d, wd_d, f"wds{e_}", e_, r0))

    def drip(n):
        while n > 0 and cast_q:
            cast_q.pop(0)()
            n -= 1

    x_rows = x_d.rearrange("(n p) d -> n p d", p=128)
    kwT3 = kwT_d.rearrange("(dc p) j -> p dc j", p=128)
    vwT3 = vwT_d.rearrange("(dc p) j -> p dc j", p=128)

    def x_stage(g, drips=True):
        for c in range(4):
            ch = g * 4 + c
            xb, xbB = x_bf[c % 2], B(f"x_bf{c % 2}")
            ev = dma("gpsimd", xb[:], x_rows[ch], waits=xbB.wdeps())
            xbB.wrote(ev)
            xt, xtB = xT[c], B(f"xT{c}")
            ev = dma("scalar", xt.rearrange("p (dc t) -> p dc t", t=128), xb[:],
                     waits=xbB.rdeps() + xtB.wdeps(), transpose=True)
            xbB.read(ev)
            xtB.wrote(ev)
        if drips:
            drip(12)

    x_stage(0, drips=False)
    transpose_weight(kw_d, kwT_d, "kwTs", range(4))
    transpose_weight(vw_d, vwT_d, "vwTs", range(4))
    transpose_weight(kw_d, kwT_d, "kwTs", range(4, NDC))
    transpose_weight(vw_d, vwT_d, "vwTs", range(4, NDC))
    tq = [(lambda rc=rc: transpose_weight(qw_d, qwT_d, "qwTs", [rc])) for rc in range(NDC)]
    tq += [(lambda rc=rc: transpose_weight(ow_d, owT_d, "owTs", [rc])) for rc in range(NDC)]

    def tdrip(n):
        while n > 0 and tq:
            tq.pop(0)()
            n -= 1

    for g in range(NG):
        if g > 0:
            x_stage(g)
            tdrip(5)
            drip(2)
        for buf2, bname, src3, stag, brow, btag, dst4, dtag, is_k in (
                (kwb, "kwb", kwT3, "kwTs", bias_k, "bias_k", kp_t, "kp_t", True),
                (vwb, "vwb", vwT3, "vwTs", bias_v, "bias_v", v_t, "v_t", False)):
            for jb in range(4):
                wb, wbB = buf2[jb % 2], B(f"{bname}{jb % 2}")
                ev = dma("sync", wb.rearrange("p (dc j) -> p dc j", j=512),
                         src3[:, :, jb * 512:(jb + 1) * 512],
                         waits=wbB.wdeps() + wT_deps(stag, jb))
                wbB.wrote(ev)
                wb3 = wb.rearrange("p (dc j) -> p dc j", j=512)
                for c in range(4):
                    xt3 = xT[c].rearrange("p (dc t) -> p dc t", t=128)
                    xtB = B(f"xT{c}")
                    dst, dstB = dst4[c], B(f"{dtag}{c}")
                    bank_i = (jb * 4 + c) % 2
                    p = ps[bank_i]
                    mms = []
                    for dc in range(NDC):
                        mms.append(lambda e, p=p, xt3=xt3, dc=dc, wb3=wb3:
                                   e.matmul(p[:], xt3[:, dc, :], wb3[:, dc, :],
                                            start=(dc == 0), stop=False))
                    mms.append(lambda e, p=p, brow=brow, jb=jb:
                               e.matmul(p[:], ones_row[0:1, 0:128], brow[0:1, jb * 512:(jb + 1) * 512],
                                        start=False, stop=True))
                    ev = mm_group(mms, bank_i, xtB.rdeps() + wbB.rdeps()
                                  + B("ones_row").rdeps() + B(btag).rdeps())
                    xtB.read(ev)
                    wbB.read(ev)
                    first = (jb == 0)
                    if is_k:
                        ev1 = sc.do("scalar", lambda e, p=p: e.activation(out=e_scr[:], in_=p[:],
                                                                          func=AF.Exp),
                                    waits=bankb[bank_i].rdeps() + B("e_scr").wdeps())
                        bankb[bank_i].read(ev1)
                        B("e_scr").wrote(ev1)
                        ev2 = sc.do("vector", lambda e: e.tensor_scalar(e_scr[:], e_scr[:], 1.0,
                                                                        None, ALU.min),
                                    waits=B("e_scr").rdeps())
                        B("e_scr").wrote(ev2, reset=False)
                        ev3 = sc.do("vector", lambda e, p=p, dst=dst, jb=jb:
                                    e.scalar_tensor_tensor(dst[:, jb * 512:(jb + 1) * 512], p[:], 0.0,
                                                           e_scr[:], ALU.max, ALU.add),
                                    waits=bankb[bank_i].rdeps() + B("e_scr").rdeps()
                                    + (dstB.wdeps() if first else []))
                        bankb[bank_i].read(ev3)
                        B("e_scr").read(ev3)
                        dstB.wrote(ev3, reset=first)
                    else:
                        ev3 = sc.do("scalar", lambda e, p=p, dst=dst, jb=jb:
                                    e.activation(out=dst[:, jb * 512:(jb + 1) * 512], in_=p[:],
                                                 func=AF.Copy),
                                    waits=bankb[bank_i].rdeps() + (dstB.wdeps() if first else []))
                        bankb[bank_i].read(ev3)
                        dstB.wrote(ev3, reset=first)
        for h in range(H):
            bi = 3 + (h % 4)
            ev = None
            for c in range(4):
                kpB, vB = B(f"kp_t{c}"), B(f"v_t{c}")
                m = (lambda e, bi=bi, h=h, kp=kp_t[c], v=v_t[c], c=c:
                     e.matmul(ps[bi][:, 0:128], kp[:, h * 128:(h + 1) * 128],
                              v[:, h * 128:(h + 1) * 128], start=(c == 0), stop=(c == 3)))
                w = kpB.rdeps() + vB.rdeps() + (bankb[bi].wdeps() if c == 0 else [])
                sc.do("tensor", m, waits=w)
            for c in range(4):
                kpB = B(f"kp_t{c}")
                m = (lambda e, bi=bi, h=h, kp=kp_t[c], c=c:
                     e.matmul(ps[bi][:, 128:129], kp[:, h * 128:(h + 1) * 128], onesc[:],
                              start=(c == 0), stop=(c == 3)))
                ev = sc.do("tensor", m, waits=kpB.rdeps() + B("onesc").rdeps())
            bankb[bi].wrote(ev, reset=True)
            for c in range(4):
                B(f"kp_t{c}").read(ev)
                B(f"v_t{c}").read(ev)
            if g == 0:
                evk = sc.do("vector", lambda e, bi=bi, h=h:
                            e.tensor_copy(kv_acc[:, h * 128:(h + 1) * 128], ps[bi][:, 0:128]),
                            waits=bankb[bi].rdeps() + (B("kv_acc").wdeps() if h == 0 else []))
                B("kv_acc").wrote(evk, reset=(h == 0))
                evs = sc.do("vector", lambda e, bi=bi, h=h:
                            e.tensor_copy(ksum_acc[:, h:h + 1], ps[bi][:, 128:129]),
                            waits=(B("ksum_acc").wdeps() if h == 0 else []))
                B("ksum_acc").wrote(evs, reset=(h == 0))
            else:
                evk = sc.do("vector", lambda e, bi=bi, h=h:
                            e.tensor_tensor(kv_acc[:, h * 128:(h + 1) * 128],
                                            kv_acc[:, h * 128:(h + 1) * 128],
                                            ps[bi][:, 0:128], ALU.add),
                            waits=bankb[bi].rdeps() + B("kv_acc").rdeps())
                B("kv_acc").wrote(evk, reset=False)
                evs = sc.do("vector", lambda e, bi=bi, h=h:
                            e.tensor_tensor(ksum_acc[:, h:h + 1], ksum_acc[:, h:h + 1],
                                            ps[bi][:, 128:129], ALU.add),
                            waits=B("ksum_acc").rdeps())
                B("ksum_acc").wrote(evs, reset=False)
            bankb[bi].read(evk)
            bankb[bi].read(evs)

    tdrip(len(tq))
    ev = sc.do("vector", lambda e: e.tensor_copy(kvz[:], kv_acc[:]), waits=B("kv_acc").rdeps())
    B("kv_acc").read(ev)
    B("kvz").wrote(ev)
    for h in range(H):
        ev = sc.do("vector", lambda e, h=h: e.tensor_copy(ksum_bd[:, h * 17:h * 17 + 1],
                                                          ksum_acc[:, h:h + 1]),
                   waits=B("ksum_acc").rdeps() + (B("ksum_bd").wdeps() if h == 0 else []))
        B("ksum_acc").read(ev)
        B("ksum_bd").wrote(ev, reset=False)
    if debug:
        ev = dma("sync", dbg_kvz, kv_acc[:], waits=B("kv_acc").rdeps())
        B("outstores").wrote(ev, reset=False)
        ev = dma("sync", dbg_ks, ksum_acc[:], waits=B("ksum_acc").rdeps())
        B("outstores").wrote(ev, reset=False)

    kvz_snap = B("kvz").rdeps()
    bd_snap = B("ksum_bd").rdeps()
    sel_snap = B("selb").rdeps()
    outst_snap = B("outstores").rdeps() if debug else []
    qwT3 = qwT_d.rearrange("(dc p) j -> p dc j", p=128)
    owT3 = owT_d.rearrange("(jc p) i -> p jc i", p=128)
    qwTs_snap = {rc: B(f"qwTs{rc}").rdeps() for rc in range(NDC)}
    owTs_snap = {rc: B(f"owTs{rc}").rdeps() for rc in range(NDC)}
    barrier()
    for nm, snap in (("kvz", kvz_snap), ("ksum_bd", bd_snap), ("selb", sel_snap),
                     ("outstores", outst_snap)):
        for evx in snap:
            B(nm).wrote(evx, reset=False)
    for rc in range(NDC):
        for evx in qwTs_snap[rc]:
            B(f"qwTs{rc}").wrote(evx, reset=False)
        for evx in owTs_snap[rc]:
            B(f"owTs{rc}").wrote(evx, reset=False)

    for ib in range(4):
        wb, wbB = owb[ib], B(f"owb{ib}")
        ev = dma("gpsimd", wb.rearrange("p (jc i) -> p jc i", i=512),
                 owT3[:, :, ib * 512:(ib + 1) * 512],
                 waits=wbB.wdeps() + wT_deps("owTs", ib))
        wbB.wrote(ev)

    qpT3 = qpT.rearrange("p (jc t) -> p jc t", t=512)
    atT3 = attnT.rearrange("p (jc t) -> p jc t", t=512)
    xTfm3 = xT_fm.rearrange("p (dc t) -> p dc t", t=512)
    y_rows = y_d.rearrange("(n p) d -> n p d", p=128)
    def bx_stage(g):
        for c in range(4):
            ch = g * 4 + c
            xb, xbB = xB_bf[c % 2], B(f"xB_bf{c % 2}")
            ev = dma("gpsimd", xb[:], x_rows[ch], waits=xbB.wdeps())
            xbB.wrote(ev)
            ev = dma("scalar", xTfm3[:, :, c * 128:(c + 1) * 128], xb[:],
                     waits=xbB.rdeps() + (B("xT_fm").wdeps() if c == 0 else []), transpose=True)
            xbB.read(ev)
            B("xT_fm").wrote(ev, reset=(c == 0))
        drip(12)

    bx_stage(0)
    for g in range(NG):
        for jb in range(4):
            wb, wbB = qwb[jb % 2], B(f"qwb{jb % 2}")
            ev = dma("sync", wb.rearrange("p (dc j) -> p dc j", j=512),
                     qwT3[:, :, jb * 512:(jb + 1) * 512], waits=wbB.wdeps() + wT_deps("qwTs", jb))
            wbB.wrote(ev)
            wb3 = wb.rearrange("p (dc j) -> p dc j", j=512)
            for jl in range(4):
                jc = jb * 4 + jl
                bank_i = jl % 2
                p = ps[bank_i]
                mms = []
                for dc in range(NDC):
                    mms.append(lambda e, p=p, dc=dc, wb3=wb3, jl=jl:
                               e.matmul(p[:], wb3[:, dc, jl * 128:(jl + 1) * 128], xTfm3[:, dc, :],
                                        start=(dc == 0), stop=False))
                mms.append(lambda e, p=p, jc=jc:
                           e.matmul(p[:], bias_q[0:1, jc * 128:(jc + 1) * 128], ones_row[0:1, 0:512],
                                    start=False, stop=True))
                ev = mm_group(mms, bank_i, B("xT_fm").rdeps() + wbB.rdeps()
                              + B("bias_q").rdeps() + B("ones_row").rdeps())
                B("xT_fm").read(ev)
                wbB.read(ev)
                ev1 = sc.do("scalar", lambda e, p=p: e.activation(out=eB_scr[:], in_=p[:], func=AF.Exp),
                            waits=bankb[bank_i].rdeps() + B("eB_scr").wdeps())
                bankb[bank_i].read(ev1)
                B("eB_scr").wrote(ev1)
                ev2 = sc.do("vector", lambda e: e.tensor_scalar(eB_scr[:], eB_scr[:], 1.0, None, ALU.min),
                            waits=B("eB_scr").rdeps())
                B("eB_scr").wrote(ev2, reset=False)
                ev3 = sc.do("vector", lambda e, p=p, jc=jc:
                            e.scalar_tensor_tensor(qpT3[:, jc, :], p[:], 0.0, eB_scr[:],
                                                   ALU.max, ALU.add),
                            waits=bankb[bank_i].rdeps() + B("eB_scr").rdeps()
                            + (B("qpT").wdeps() if jc == 0 else []))
                bankb[bank_i].read(ev3)
                B("eB_scr").read(ev3)
                B("qpT").wrote(ev3, reset=(jc == 0))
        if g + 1 < NG:
            bx_stage(g + 1)
        mms = []
        for jc in range(NDC):
            mms.append(lambda e, jc=jc:
                       e.matmul(ps[2][0:16, :], ksum_bd[:, jc * 16:(jc + 1) * 16], qpT3[:, jc, :],
                                start=(jc == 0), stop=(jc == NDC - 1)))
        ev = mm_group(mms, 2, B("qpT").rdeps() + B("ksum_bd").rdeps())
        B("qpT").read(ev)
        ev1 = sc.do("vector", lambda e: e.tensor_scalar(zs[:], ps[2][0:16, :], ZEPS, None, ALU.add),
                    waits=bankb[2].rdeps() + B("zs").wdeps())
        bankb[2].read(ev1)
        B("zs").wrote(ev1)
        ev2 = sc.do("vector", lambda e: e.reciprocal(zs[:], zs[:]), waits=B("zs").rdeps())
        B("zs").wrote(ev2, reset=False)
        ev3 = sc.do("vector", lambda e: e.tensor_copy(zrec_bf[:], zs[:]),
                    waits=B("zs").rdeps() + B("zrec_bf").wdeps())
        B("zs").read(ev3)
        B("zrec_bf").wrote(ev3)
        for h in range(H):
            ev = mm_group([lambda e, h=h: e.matmul(ps[6][:], selb[:, h * 128:(h + 1) * 128],
                                                   zrec_bf[:], start=True, stop=True)],
                          6, B("selb").rdeps() + B("zrec_bf").rdeps())
            B("zrec_bf").read(ev)
            B("selb").read(ev)
            evq = sc.do("vector", lambda e, h=h: e.tensor_tensor(qpT3[:, h, :], qpT3[:, h, :],
                                                                 ps[6][:], ALU.mult),
                        waits=bankb[6].rdeps() + B("qpT").wdeps())
            bankb[6].read(evq)
            B("qpT").wrote(evq, reset=False)
            bank_i = 4 + h % 2
            ev = mm_group([lambda e, h=h, p=ps[bank_i]:
                           e.matmul(p[:], kvz[:, h * 128:(h + 1) * 128], qpT3[:, h, :],
                                    start=True, stop=True)],
                          bank_i, B("kvz").rdeps() + [evq])
            B("qpT").read(ev)
            B("kvz").read(ev)
            eva = sc.do("scalar", lambda e, h=h, p=ps[bank_i]:
                        e.activation(out=atT3[:, h, :], in_=p[:], func=AF.Copy),
                        waits=bankb[bank_i].rdeps() + (B("attnT").wdeps() if h == 0 else []))
            bankb[bank_i].read(eva)
            B("attnT").wrote(eva, reset=(h == 0))
        for c in range(4):
            ch = g * 4 + c
            evx = dma("scalar", x_f32[:], x_rows[ch], waits=B("x_f32").wdeps())
            B("x_f32").wrote(evx)
            for ib in range(4):
                wb, wbB = owb[ib], B(f"owb{ib}")
                wb3 = wb.rearrange("p (jc i) -> p jc i", i=512)
                bank_i = ib % 2
                p = ps[bank_i]
                mms = []
                for jc in range(NDC):
                    mms.append(lambda e, p=p, jc=jc, wb3=wb3, c=c:
                               e.matmul(p[:], atT3[:, jc, c * 128:(c + 1) * 128], wb3[:, jc, :],
                                        start=(jc == 0), stop=False))
                mms.append(lambda e, p=p, ib=ib:
                           e.matmul(p[:], ones_row[0:1, 0:128], bias_o[0:1, ib * 512:(ib + 1) * 512],
                                    start=False, stop=True))
                ev = mm_group(mms, bank_i, B("attnT").rdeps() + wbB.rdeps()
                              + B("bias_o").rdeps() + B("ones_row").rdeps())
                B("attnT").read(ev)
                wbB.read(ev)
                evy = sc.do("vector", lambda e, p=p, ib=ib:
                            e.tensor_tensor(y_f32[:, ib * 512:(ib + 1) * 512], p[:],
                                            x_f32[:, ib * 512:(ib + 1) * 512], ALU.add),
                            waits=bankb[bank_i].rdeps() + B("x_f32").rdeps()
                            + (B("y_f32").wdeps() if ib == 0 else []))
                bankb[bank_i].read(evy)
                B("x_f32").read(evy)
                B("y_f32").wrote(evy, reset=(ib == 0))
            ev1 = sc.do("scalar", lambda e: e.activation(out=x_f32[:], in_=y_f32[:], func=AF.Copy,
                                                         accum_out=sA[:]),
                        waits=B("y_f32").rdeps() + B("x_f32").wdeps() + B("sA").wdeps())
            B("y_f32").read(ev1)
            B("x_f32").wrote(ev1)
            B("sA").wrote(ev1)
            ev2 = sc.do("scalar", lambda e: e.activation(out=x_f32[:], in_=y_f32[:], func=AF.Square,
                                                         accum_out=sB[:]),
                        waits=B("sB").wdeps())
            B("y_f32").read(ev2)
            B("x_f32").wrote(ev2, reset=False)
            B("sB").wrote(ev2)
            ln_apply(y_f32[:], y_f32[:], "y_f32", "y_f32")
            ev = dma("scalar", y_rows[ch], y_f32[:], waits=B("y_f32").rdeps())
            B("y_f32").read(ev)
            B("yscr").wrote(ev, reset=False)
            if debug:
                ev = dma("sync", dbg_y.rearrange("(n p) d -> n p d", p=128)[ch], y_f32[:],
                         waits=B("y_f32").rdeps())
                B("y_f32").read(ev)
                B("outstores").wrote(ev, reset=False)

    drip(len(cast_q))
    y_snap = B("yscr").rdeps()
    wrb_snap = B("wrb").rdeps()
    invt_snap = B("invt").rdeps()
    outst_snap = B("outstores").rdeps() if debug else []
    wsnap = {}
    for e_ in range(NE):
        wsnap[("g", e_)] = B(f"wgs{e_}").rdeps()
        wsnap[("u", e_)] = B(f"wus{e_}").rdeps()
        wsnap[("d", e_)] = B(f"wds{e_}").rdeps()
    wgs3 = [wgs_d[e_].rearrange("(dc p) f -> p dc f", p=128) for e_ in range(NE)]
    wus3 = [wus_d[e_].rearrange("(dc p) f -> p dc f", p=128) for e_ in range(NE)]
    wds3 = [wds_d[e_].rearrange("(fc p) i -> p fc i", p=128) for e_ in range(NE)]
    barrier()
    for evx in y_snap:
        B("yscr").wrote(evx, reset=False)
    for evx in wrb_snap:
        B("wrb").wrote(evx, reset=False)
    for evx in invt_snap:
        B("invt").wrote(evx, reset=False)
    for evx in outst_snap:
        B("outstores").wrote(evx, reset=False)
    for key, snap in wsnap.items():
        kind, e_ = key
        nm = {"g": "wgs", "u": "wus", "d": "wds"}[kind] + str(e_)
        for evx in snap:
            B(nm).wrote(evx, reset=False)

    yT3 = [yT[i].rearrange("p (dc t) -> p dc t", t=512) for i in range(2)]
    oa3 = out_acc.rearrange("p (tc d) -> p tc d", d=D)
    h3 = h_sb.rearrange("p (fc t) -> p fc t", t=512)
    out_rows = out_d.rearrange("(n p) d -> n p d", p=128)

    def oa_init(g):
        for c in range(4):
            ch = g * 4 + c
            ev = dma("scalar", oa3[:, c, :], y_rows[ch],
                     waits=B("out_acc").wdeps() + B("yscr").rdeps())
            B("out_acc").wrote(ev, reset=(c == 0))

    def yt_load(g, c):
        ybB = B("y_bf")
        ev = dma("gpsimd", y_bf[:], y_rows[g * 4 + c], waits=ybB.wdeps() + B("yscr").rdeps())
        ybB.wrote(ev)

    def yt_tr(g, c, queue):
        par = g % 2
        ybB = B("y_bf")
        ev = dma(queue, yT3[par][:, :, c * 128:(c + 1) * 128], y_bf[:],
                 waits=ybB.rdeps() + (B(f"yT{par}").wdeps() if c == 0 else []),
                 transpose=True)
        ybB.read(ev)
        B(f"yT{par}").wrote(ev, reset=(c == 0))

    def yt_stage(g):
        for c in range(4):
            yt_load(g, c)
            yt_tr(g, c, "scalar")

    def router_chunk(g, c):
        par = g % 2
        mms = []
        for dc in range(NDC):
            mms.append(lambda e, dc=dc, c=c, par=par:
                       e.matmul(ps[2][:, 0:NE], yT3[par][:, dc, c * 128:(c + 1) * 128],
                                wrb[:, dc * NE:(dc + 1) * NE],
                                start=(dc == 0), stop=(dc == NDC - 1)))
        ev = mm_group(mms, 2, B(f"yT{par}").rdeps() + B("wrb").rdeps())
        B(f"yT{par}").read(ev)
        router(ps[2][:, 0:NE], w_all[par][:, c * NE:(c + 1) * NE], c == 0, f"w_all{par}")

    oa_init(0)
    yt_stage(0)
    for c in range(4):
        router_chunk(0, c)
    for g in range(NG):
        par = g % 2
        for e_ in range(NE):
            if e_ >= 4 and g + 1 < NG:
                router_chunk(g + 1, e_ - 4)
            for fb in range(4):
                if e_ == 0 and g + 1 < NG:
                    yt_load(g + 1, fb)
                    yt_tr(g + 1, fb, "sync")
                gb, gbB = wgb[fb % 2], B(f"wgb{fb % 2}")
                ev = dma("sync", gb.rearrange("p (dc f) -> p dc f", f=512),
                         wgs3[e_][:, :, fb * 512:(fb + 1) * 512],
                         waits=gbB.wdeps() + B(f"wgs{e_}").rdeps())
                gbB.wrote(ev)
                ub, ubB = wub[fb % 2], B(f"wub{fb % 2}")
                ev = dma("sync", ub.rearrange("p (dc f) -> p dc f", f=512),
                         wus3[e_][:, :, fb * 512:(fb + 1) * 512],
                         waits=ubB.wdeps() + B(f"wus{e_}").rdeps())
                ubB.wrote(ev)
                gb3 = gb.rearrange("p (dc f) -> p dc f", f=512)
                ub3 = ub.rearrange("p (dc f) -> p dc f", f=512)
                for jf in range(4):
                    fg = fb * 4 + jf
                    mmsg = []
                    mmsu = []
                    for dc in range(NDC):
                        mmsg.append(lambda e, dc=dc, jf=jf, gb3=gb3, par=par:
                                    e.matmul(ps[0][:], gb3[:, dc, jf * 128:(jf + 1) * 128],
                                             yT3[par][:, dc, :], start=(dc == 0), stop=(dc == NDC - 1)))
                        mmsu.append(lambda e, dc=dc, jf=jf, ub3=ub3, par=par:
                                    e.matmul(ps[1][:], ub3[:, dc, jf * 128:(jf + 1) * 128],
                                             yT3[par][:, dc, :], start=(dc == 0), stop=(dc == NDC - 1)))
                    evg = mm_group(mmsg, 0, B(f"yT{par}").rdeps() + gbB.rdeps())
                    evu = mm_group(mmsu, 1, B(f"yT{par}").rdeps() + ubB.rdeps())
                    B(f"yT{par}").read(evu)
                    gbB.read(evg)
                    ubB.read(evu)
                    evs = sc.do("scalar", lambda e: e.activation(out=s_sb[:], in_=ps[0][:], func=AF.Silu),
                                waits=bankb[0].rdeps() + B("s_sb").wdeps())
                    bankb[0].read(evs)
                    B("s_sb").wrote(evs)
                    evh = sc.do("vector", lambda e, fg=fg: e.tensor_tensor(h3[:, fg, :], ps[1][:],
                                                                           s_sb[:], ALU.mult),
                                waits=bankb[1].rdeps() + B("s_sb").rdeps()
                                + (B("h_sb").wdeps() if fg == 0 else []))
                    bankb[1].read(evh)
                    B("s_sb").read(evh)
                    B("h_sb").wrote(evh, reset=(fg == 0))
            for ib in range(4):
                db, dbB = wdb[ib % 2], B(f"wdb{ib % 2}")
                ev = dma("sync", db.rearrange("p (fc i) -> p fc i", i=512),
                         wds3[e_][:, :, ib * 512:(ib + 1) * 512],
                         waits=dbB.wdeps() + B(f"wds{e_}").rdeps())
                dbB.wrote(ev)
                db3 = db.rearrange("p (fc i) -> p fc i", i=512)
                for tc in range(4):
                    bank_i = 3 + tc
                    p = ps[bank_i]
                    mms = []
                    for fc in range(NDC):
                        mms.append(lambda e, p=p, fc=fc, tc=tc, db3=db3:
                                   e.matmul(p[:], h3[:, fc, tc * 128:(tc + 1) * 128], db3[:, fc, :],
                                            start=(fc == 0), stop=(fc == NDC - 1)))
                    ev = mm_group(mms, bank_i, B("h_sb").rdeps() + dbB.rdeps())
                    B("h_sb").read(ev)
                    dbB.read(ev)
                    evo = sc.do("vector", lambda e, p=p, tc=tc, ib=ib, e_=e_, par=par:
                                e.scalar_tensor_tensor(oa3[:, tc, ib * 512:(ib + 1) * 512], p[:],
                                                       w_all[par][:, tc * NE + e_:tc * NE + e_ + 1],
                                                       oa3[:, tc, ib * 512:(ib + 1) * 512],
                                                       ALU.mult, ALU.add),
                                waits=bankb[bank_i].rdeps() + B(f"w_all{par}").rdeps()
                                + B("out_acc").rdeps())
                    bankb[bank_i].read(evo)
                    B(f"w_all{par}").read(evo)
                    B("out_acc").wrote(evo, reset=False)
        for c in range(4):
            ch = g * 4 + c
            ev1 = sc.do("scalar", lambda e, c=c: e.activation(out=o_f32[:], in_=oa3[:, c, :],
                                                              func=AF.Copy, accum_out=sA[:]),
                        waits=B("out_acc").rdeps() + B("o_f32").wdeps() + B("sA").wdeps())
            B("out_acc").read(ev1)
            B("o_f32").wrote(ev1)
            B("sA").wrote(ev1)
            ev2 = sc.do("scalar", lambda e, c=c: e.activation(out=o_f32[:], in_=oa3[:, c, :],
                                                              func=AF.Square, accum_out=sB[:]),
                        waits=B("sB").wdeps())
            B("out_acc").read(ev2)
            B("o_f32").wrote(ev2, reset=False)
            B("sB").wrote(ev2)
            ln_apply(oa3[:, c, :], o_f32[:], "out_acc", "o_f32")
            ev = dma("scalar", out_rows[ch], o_f32[:], waits=B("o_f32").rdeps())
            B("o_f32").read(ev)
            B("outstores").wrote(ev, reset=False)
        if g + 1 < NG:
            oa_init(g + 1)

    final = B("outstores").rdeps()
    sc.steps["sync"].append((sorted(dict(final).items()), None, None, 0))

    from contextlib import ExitStack
    es = ExitStack()
    sems = {n: es.enter_context(nc.semaphore(n)) for n in sorted(sc.cnt.keys())}
    block = es.enter_context(nc.Block())

    def mk_body(engname):
        def body(eng):
            for ws, emit, sem, amt in sc.steps[engname]:
                for s, v in ws:
                    eng.wait_ge(sems[s], v)
                if emit is not None:
                    inst = emit(eng)
                    if sem is not None:
                        inst.then_inc(sems[sem], amt)
        return body

    for engname in ENGINES:
        getattr(block, engname)(mk_body(engname))
    es.close()
    build_program._last_sch = sc
    return nc


_NC_CACHE = {}


def _sel_mat():
    return np.kron(np.eye(16, dtype=np.float32), np.ones((1, 128), np.float32))


def kernel(L0_attn_hidden_states, L0_attn_q_w, L0_attn_q_b, L0_attn_k_w, L0_attn_k_b,
           L0_attn_v_w, L0_attn_v_b, L0_attn_o_w, L0_attn_o_b,
           L0_ffn_Wg, L0_ffn_Wu, L0_ffn_Wd, L0_ffn_Wr, L0_ffn_temperature):
    x = np.ascontiguousarray(np.asarray(L0_attn_hidden_states, dtype=np.float32))
    Bn, S, Dm = x.shape
    assert Dm == D
    key = S
    if key not in _NC_CACHE:
        _NC_CACHE[key] = build_program(S)
    nc = _NC_CACHE[key]
    common = {
        "qw": np.ascontiguousarray(np.asarray(L0_attn_q_w, dtype=np.float32)),
        "kw": np.ascontiguousarray(np.asarray(L0_attn_k_w, dtype=np.float32)),
        "vw": np.ascontiguousarray(np.asarray(L0_attn_v_w, dtype=np.float32)),
        "ow": np.ascontiguousarray(np.asarray(L0_attn_o_w, dtype=np.float32)),
        "qb": np.ascontiguousarray(np.asarray(L0_attn_q_b, dtype=np.float32).reshape(1, D)),
        "kb": np.ascontiguousarray(np.asarray(L0_attn_k_b, dtype=np.float32).reshape(1, D)),
        "vb": np.ascontiguousarray(np.asarray(L0_attn_v_b, dtype=np.float32).reshape(1, D)),
        "ob": np.ascontiguousarray(np.asarray(L0_attn_o_b, dtype=np.float32).reshape(1, D)),
        "wg": np.ascontiguousarray(np.asarray(L0_ffn_Wg, dtype=np.float32)),
        "wu": np.ascontiguousarray(np.asarray(L0_ffn_Wu, dtype=np.float32)),
        "wd": np.ascontiguousarray(np.asarray(L0_ffn_Wd, dtype=np.float32)),
        "wr": np.ascontiguousarray(np.asarray(L0_ffn_Wr, dtype=np.float32)),
        "temp": np.ascontiguousarray(np.asarray(L0_ffn_temperature, dtype=np.float32).reshape(1, 1)),
        "sel": _sel_mat(),
    }
    in_maps = [dict(common, x=np.ascontiguousarray(x[b])) for b in range(Bn)]
    res = run_bass_kernel_spmd(nc, in_maps, core_ids=list(range(Bn)))
    return np.stack([np.asarray(r["out"], dtype=np.float32) for r in res.results], axis=0)
